# version 29
# baseline (speedup 1.0000x reference)
import contextlib
import numpy as np
import ml_dtypes
import concourse.bass as bass
import concourse.mybir as mybir
from concourse.bass_utils import run_bass_kernel_spmd

F32 = mybir.dt.float32
BF16 = mybir.dt.bfloat16
I32 = mybir.dt.int32
AF = mybir.ActivationFunctionType
ALU = mybir.AluOpType
AX = mybir.AxisListType

T = 2048
D = 1024
FF = 4096
NT = 16
ND = 8
NF = 32
EPS = 1e-5
NEG_INF = -1e30
PAIRS = ((0, 0), (0, 1), (1, 0))


class Sem:
    def __init__(self, nc, es, name):
        self.h = es.enter_context(nc.semaphore(name))
        self.n = 0

    def inc(self, instr, k=1):
        instr.then_inc(self.h, k)
        self.n += k
        return self.n


def build_nc():
    nc = bass.Bass("TRN2", target_bir_lowering=False)
    es = contextlib.ExitStack()

    x_h = nc.dram_tensor("x", [T, D], F32, kind="ExternalInput").ap()
    wq_h = nc.dram_tensor("wq", [128, ND * D], F32, kind="ExternalInput").ap()
    wk_h = nc.dram_tensor("wk", [128, ND * D], F32, kind="ExternalInput").ap()
    wv_h = nc.dram_tensor("wv", [128, 2 * 4096], F32, kind="ExternalInput").ap()
    bqT_h = nc.dram_tensor("bqT", [128, ND], F32, kind="ExternalInput").ap()
    bkT_h = nc.dram_tensor("bkT", [128, ND], F32, kind="ExternalInput").ap()
    b1T_h = nc.dram_tensor("b1T", [128, NF], F32, kind="ExternalInput").ap()
    gam_h = nc.dram_tensor("gam", [128, D], F32, kind="ExternalInput").ap()
    bet_h = nc.dram_tensor("bet", [128, D], F32, kind="ExternalInput").ap()
    bvb_h = nc.dram_tensor("bvb", [128, D], F32, kind="ExternalInput").ap()
    b2b_h = nc.dram_tensor("b2b", [128, D], F32, kind="ExternalInput").ap()
    q1_h = nc.dram_tensor("q1", [ND, 8, 128, 512], I32, kind="ExternalInput").ap()
    s1_h = nc.dram_tensor("s1", [ND, 8, 128, 512], F32, kind="ExternalInput").ap()
    q2_h = nc.dram_tensor("q2", [FF, D], I32, kind="ExternalInput").ap()
    s2_h = nc.dram_tensor("s2", [NF, 128, D], F32, kind="ExternalInput").ap()
    mask_h = nc.dram_tensor("mask", [128, 128], F32, kind="ExternalInput").ap()
    eyeb_h = nc.dram_tensor("eyeb", [128, 128], BF16, kind="ExternalInput").ap()
    out_h = nc.dram_tensor("out", [T, D], F32, kind="ExternalOutput").ap()

    hxn = [nc.dram_tensor(f"hxn{i}", [T, D], BF16, kind="Internal").ap() for i in range(2)]
    hh = [nc.dram_tensor(f"hh{i}", [T, D], BF16, kind="Internal").ap() for i in range(2)]
    hqT = nc.dram_tensor("hqT", [NT, 2, 128, D], BF16, kind="Internal").ap()
    hattn = nc.dram_tensor("hattn", [NT, 128, D], F32, kind="Internal").ap()
    hy = nc.dram_tensor("hy", [4, 2, 2, 128, NF, 256], BF16, kind="Internal").ap()

    _rsv = nc.alloc_sbuf_tensor("arena_reserve", [128, 208704], mybir.dt.uint8)
    BASE = None
    for _a in nc.m.functions[0].allocations:
        if isinstance(_a, mybir.MemoryLocationSet) and _a.memorylocations[0].name == "arena_reserve":
            BASE = _a.memorylocations[0].addr
    assert BASE is not None and BASE + 208704 <= 229376, BASE

    def sb(name, offset, shape, dt):
        return nc.alloc_sbuf_tensor_at(name, shape, dt, offset=BASE + offset).ap()

    A_x = sb("A_x", 0, [128, NT * D], F32)
    A_kth = sb("A_kth", 0, [128, ND * T], BF16)
    A_ktl = sb("A_ktl", 32768, [128, ND * T], BF16)
    A_w1h = sb("A_w1h", 0, [128, ND * FF], BF16)
    A_w2h = sb("A_w2h", 0, [128, NF * D], BF16)

    C_wst = sb("C_wst", 65536, [128, ND * D], F32)
    C_wh = sb("C_wh", 98304, [128, ND * D], BF16)
    C_wl = sb("C_wl", 114688, [128, ND * D], BF16)
    C_vh = sb("C_vh", 65536, [128, NT * D], BF16)
    C_vl = sb("C_vl", 98304, [128, NT * D], BF16)
    C_w1l = sb("C_w1l", 65536, [128, ND * FF], BF16)
    C_w2l = sb("C_w2l", 65536, [128, NF * D], BF16)

    B_sh = [sb(f"B_sh{i}", 184320 + i * 2048, [128, D], BF16) for i in range(2)]
    B_sl = [sb(f"B_sl{i}", 188416 + i * 2048, [128, D], BF16) for i in range(2)]
    B_xph = [sb(f"B_xph{i}", 176128 + i * 2048, [128, D], BF16) for i in range(2)]
    B_xpl = [sb(f"B_xpl{i}", 180224 + i * 2048, [128, D], BF16) for i in range(2)]
    B_tf = sb("B_tf", 192512, [128, D], F32)
    B_xth = [sb(f"B_xth{i}", 131072 + i * 16384, [128, ND * 512], BF16) for i in range(2)]
    B_xtl = [sb(f"B_xtl{i}", 139264 + i * 16384, [128, ND * 512], BF16) for i in range(2)]
    B_qsh = sb("B_qsh", 167936, [128, ND * 512], BF16)
    B_qsl = sb("B_qsl", 176128, [128, ND * 512], BF16)
    B_wvh = sb("B_wvh", 163840, [128, ND * D], BF16)
    B_wvl = sb("B_wvl", 180224, [128, ND * D], BF16)

    B_sc = sb("B_sc", 131072, [128, T], F32)
    B_pf = [sb(f"B_pf{i}", 139264 + i * 2048, [128, 512], F32) for i in range(2)]
    B_ph = sb("B_ph", 143360, [128, T], BF16)
    B_pl = sb("B_pl", 147456, [128, T], BF16)
    B_pth = sb("B_pth", 151552, [128, T], BF16)
    B_ptl = sb("B_ptl", 155648, [128, T], BF16)
    B_qth = [sb(f"B_qth{i}", 159744 + i * 2048, [128, D], BF16) for i in range(2)]
    B_qtl = [sb(f"B_qtl{i}", 163840 + i * 2048, [128, D], BF16) for i in range(2)]
    B_ast = [sb(f"B_ast{i}", 167936 + i * 4096, [128, D], F32) for i in range(2)]

    B_hth = [sb(f"B_hth{i}", 131072 + i * 16384, [128, ND * 512], BF16) for i in range(2)]
    B_htl = [sb(f"B_htl{i}", 139264 + i * 16384, [128, ND * 512], BF16) for i in range(2)]
    B_yf = [sb(f"B_yf{i}", 163840 + i * 2048, [128, 512], F32) for i in range(2)]
    B_yst = [[sb(f"B_yst{s}{h}", 167936 + (s * 2 + h) * 4096, [128, 8 * 256], BF16)
              for h in range(2)] for s in range(2)]
    B_dqq = [sb(f"B_dqq{i}", [184320, 186368, 192512][i], [128, 512], I32) for i in range(3)]
    B_dqs = [sb(f"B_dqs{i}", [188416, 190464, 194560][i], [128, 512], F32) for i in range(3)]

    B_ygh = sb("B_ygh", 131072, [128, NF * 256], BF16)
    B_ygl = sb("B_ygl", 147456, [128, NF * 256], BF16)
    B_xt2 = [sb(f"B_xt2{i}", 163840 + i * 4096, [128, D], F32) for i in range(2)]
    B_at2 = [sb(f"B_at2{i}", 172032 + i * 4096, [128, D], F32) for i in range(2)]
    B_t0 = [sb(f"B_t0{i}", 180224 + i * 2048, [128, 512], F32) for i in range(2)]
    B_zb = [sb(f"B_zb{i}", 184320 + i * 2048, [128, 512], F32) for i in range(2)]
    B_ost = [sb(f"B_ost{i}", 188416 + i * 4096, [128, D], F32) for i in range(2)]
    B_dq2q = [sb(f"B_dq2q{i}", 163840 + i * 4096, [128, D], I32) for i in range(2)]
    B_dq2s = [sb(f"B_dq2s{i}", 172032 + i * 4096, [128, D], F32) for i in range(2)]
    B_dq2f = [sb(f"B_dq2f{i}", 180224 + i * 4096, [128, D], F32) for i in range(2)]

    M_mask = sb("M_mask", 196608, [128, 128], F32)
    M_eyeb = sb("M_eyeb", 197120, [128, 128], BF16)
    off = [197376]

    def st16(name, w=16):
        a = sb(name, off[0], [128, w], F32)
        off[0] += max(64, w * 4)
        return a

    M_sum1 = st16("M_sum1"); M_mu1 = st16("M_mu1"); M_ssa = st16("M_ssa")
    M_ssb = st16("M_ssb"); M_ss = st16("M_ss"); M_v1 = st16("M_v1")
    M_vv1 = st16("M_vv1"); M_rv1 = st16("M_rv1"); M_r1 = st16("M_r1")
    M_sum2 = st16("M_sum2"); M_mu2 = st16("M_mu2")
    M_vv2 = st16("M_vv2"); M_rv2 = st16("M_rv2"); M_r2 = st16("M_r2")
    M_negm = st16("M_negm"); M_m16 = st16("M_m16"); M_z16 = st16("M_z16")
    M_e16 = st16("M_e16"); M_c16 = st16("M_c16"); M_g16 = st16("M_g16")
    M_zc = st16("M_zc")
    M_msq = st16("M_msq"); M_msq2 = st16("M_msq2")
    M_col = st16("M_col", 1)
    M_negmg = st16("M_negmg", 1)
    M_Zb = st16("M_Zb", 1)
    M_rZb = st16("M_rZb", 1)
    M_bq32 = st16("M_bq32", ND)
    M_bkT = st16("M_bkT", ND)
    M_b1T = st16("M_b1T", NF)
    M_m11 = sb("M_m11", off[0], [1, 2], F32); off[0] += 64
    M_rows = sb("M_rows", off[0], [1, 128], F32); off[0] += 512
    M_pones = sb("M_pones", off[0], [1, 128], F32); off[0] += 512
    assert off[0] <= 200448, off[0]
    M_gam = sb("M_gam", 200448, [128, D], F32)
    M_bet = sb("M_bet", 204544, [128, D], F32)
    M_bvb = sb("M_bvb", 200448, [128, D], F32)
    M_b2b = sb("M_b2b", 204544, [128, D], F32)

    P = [nc.alloc_psum_tensor(f"P{i}", [128, 512], F32).ap() for i in range(6)]
    PTp = nc.alloc_psum_tensor("PTp", [128, 1024], BF16).ap()
    PS = nc.alloc_psum_tensor("PSp", [128, 512], F32).ap()

    ve = Sem(nc, es, "ve")
    ac = Sem(nc, es, "ac")
    gp = Sem(nc, es, "gp")
    mm = Sem(nc, es, "mm")
    l0 = Sem(nc, es, "l0")
    l0g = [Sem(nc, es, f"l0g{i}") for i in range(4)]
    lw = Sem(nc, es, "lw")
    lx = [Sem(nc, es, f"lx{i}") for i in range(2)]
    lq = [Sem(nc, es, f"lq{i}") for i in range(2)]
    lh = [Sem(nc, es, f"lh{i}") for i in range(2)]
    ldq = [Sem(nc, es, f"ldq{i}") for i in range(3)]
    ly = Sem(nc, es, "ly")
    lyq = [Sem(nc, es, f"lyq{i}") for i in range(4)]
    lxa = Sem(nc, es, "lxa")
    mmq = Sem(nc, es, "mmq")
    sxg = [Sem(nc, es, f"sxg{i}") for i in range(4)]
    aev = Sem(nc, es, "aev")
    lev = Sem(nc, es, "lev")
    lxp = Sem(nc, es, "lxp")
    shh = [Sem(nc, es, f"shh{i}") for i in range(2)]
    sq = Sem(nc, es, "sq")
    sa = [Sem(nc, es, f"sa{i}") for i in range(2)]
    sy4 = [Sem(nc, es, f"sy{i}") for i in range(4)]
    so = Sem(nc, es, "so")

    def vch(vector, instr):
        v = ve.inc(instr)
        vector.wait_ge(ve.h, v)

    def rd(eng, sem, out, in_, **kw):
        eng.dma_start(out=out, in_=in_, **kw).then_inc(sem.h, 16)
        sem.n += 16
        return sem.n

    def wr(eng, sem, out, in_):
        eng.dma_start(out=out, in_=in_).then_inc(sem.h, 16)
        sem.n += 16
        return sem.n

    with nc.Block() as block:
        @block.gpsimd
        def _(gpsimd):
            rd(gpsimd, l0, M_mask[:], mask_h)
            rd(gpsimd, l0, M_eyeb[:], eyeb_h)
            rd(gpsimd, l0, M_bq32[:], bqT_h)
            rd(gpsimd, l0, M_bkT[:], bkT_h)
            rd(gpsimd, l0, M_b1T[:], b1T_h)
            rd(gpsimd, l0, M_gam[:], gam_h)
            rd(gpsimd, l0, M_bet[:], bet_h)
            for j in range(4):
                rd(gpsimd, l0g[0], A_x[:, j * D:(j + 1) * D], x_h[j * 128:(j + 1) * 128, :])
            rd(gpsimd, lw, C_wst[:], wq_h)
            for j in range(4, NT):
                rd(gpsimd, l0g[j // 4], A_x[:, j * D:(j + 1) * D], x_h[j * 128:(j + 1) * 128, :])
        lw_wq = lw.n

        @block.vector
        def _(vector):
            vector.memset(M_rows[:], -1.0)
            vector.memset(M_pones[:], 1.0)

        ve_xnt = {}
        ac_r1g = {}
        mm_tb = {}
        xtb_evac = {}

        for g4 in range(4):
            gs = slice(g4 * 4, g4 * 4 + 4)

            @block.vector
            def _(vector, g4=g4):
                vector.wait_ge(l0g[g4].h, 64)
                if g4 == 0:
                    vector.wait_ge(l0.h, 7 * 16)
                for j in range(g4 * 4, g4 * 4 + 4):
                    i = vector.reduce_sum(M_sum1[:, j:j + 1], A_x[:, j * D:(j + 1) * D], axis=AX.X)
                ve.inc(i)
            ve_s1g = ve.n

            @block.scalar
            def _(scalar, g4=g4, gs=gs, ve_s1g=ve_s1g):
                scalar.wait_ge(l0g[g4].h, 64)
                for j in range(g4 * 4, g4 * 4 + 4):
                    if not (g4 == 0 and j == 0):
                        scalar.wait_ge(ac.h, ac.n)
                    scalar.activation(P[4][:, 0:512], A_x[:, j * D:j * D + 512], AF.Square,
                                      accum_out=M_ssa[:, j:j + 1])
                    i = scalar.activation(P[5][:, 0:512], A_x[:, j * D + 512:(j + 1) * D], AF.Square,
                                          accum_out=M_ssb[:, j:j + 1])
                    ac.inc(i)
                scalar.wait_ge(ve.h, ve_s1g)
                ac.inc(scalar.activation(M_mu1[:, gs], M_sum1[:, gs], AF.Copy, scale=1.0 / 1024.0))
            ac_mug = ac.n

            @block.vector
            def _(vector, gs=gs, ac_mug=ac_mug):
                vector.wait_ge(ac.h, ac_mug)
                vch(vector, vector.tensor_tensor(M_msq[:, gs], M_mu1[:, gs], M_mu1[:, gs],
                                                 ALU.mult))
                vch(vector, vector.tensor_add(M_ss[:, gs], M_ssa[:, gs], M_ssb[:, gs]))
                vch(vector, vector.scalar_tensor_tensor(M_v1[:, gs], M_ss[:, gs], 1.0 / 1024.0,
                                                        M_msq[:, gs], ALU.mult, ALU.subtract))
                ve.inc(vector.tensor_scalar_add(M_v1[:, gs], M_v1[:, gs], EPS))
            ve_v1g = ve.n

            @block.scalar
            def _(scalar, gs=gs, ve_v1g=ve_v1g):
                scalar.wait_ge(ve.h, ve_v1g)
                ac.inc(scalar.activation(M_vv1[:, gs], M_v1[:, gs], AF.Sqrt))
            ac_r1g[g4] = ac.n

            @block.vector
            def _(vector, g4=g4, gs=gs):
                vector.wait_ge(ac.h, ac_r1g[g4])
                vch(vector, vector.reciprocal(M_r1[:, gs], M_vv1[:, gs]))

            for j in range(g4 * 4, g4 * 4 + 4):
                @block.vector
                def _(vector, j=j):
                    s = j % 2
                    if j >= 2:
                        t = j - 1
                        vector.wait_ge(sxg[t // 4].h, 32 * ((t % 4) + 1))
                        if (j - 2) // 4 != t // 4:
                            vector.wait_ge(sxg[(j - 2) // 4].h, 128)
                        if (j - 2, 1) in mm_tb:
                            vector.wait_ge(mm.h, mm_tb[(j - 2, 1)])
                    aj = A_x[:, j * D:(j + 1) * D]
                    vch(vector, vector.tensor_scalar(aj, aj, M_mu1[:, j:j + 1], M_r1[:, j:j + 1],
                                                     ALU.subtract, ALU.mult))
                    vch(vector, vector.tensor_copy(B_sh[s][:], aj))
                    i = vector.tensor_tensor(B_sl[s][:], aj, B_sh[s][:], ALU.subtract)
                    ve.inc(i)
                    ve_xnt[j] = ve.n

                if j < 4:
                    for half in range(2):
                        @block.tensor
                        def _(tensor, j=j, half=half):
                            tensor.wait_ge(ve.h, ve_xnt[j])
                            prev = (j, half - 1) if half == 1 else (j - 1, 1)
                            if prev in xtb_evac:
                                tensor.wait_ge(ve.h, xtb_evac[prev])
                            srcb = (B_sh, B_sl)[half][j % 2]
                            for g in range(ND):
                                i = tensor.transpose(PTp[:, g * 128:(g + 1) * 128],
                                                     srcb[:, g * 128:(g + 1) * 128], M_eyeb[:])
                            mm.inc(i)
                            mm_tb[(j, half)] = mm.n

                        @block.vector
                        def _(vector, j=j, half=half):
                            vector.wait_ge(mm.h, mm_tb[(j, half)])
                            dstb = (B_xth, B_xtl)[half][0]
                            for g in range(ND):
                                i = vector.tensor_copy(
                                    dstb[:, g * 512 + j * 128:g * 512 + (j + 1) * 128],
                                    PTp[:, g * 128:(g + 1) * 128])
                            ve.inc(i)
                            xtb_evac[(j, half)] = ve.n

                @block.scalar
                def _(scalar, j=j, g4=g4):
                    s = j % 2
                    scalar.wait_ge(ve.h, ve_xnt[j])
                    wr(scalar, sxg[g4], hxn[0][j * 128:(j + 1) * 128, :], B_sh[s][:])
                    wr(scalar, sxg[g4], hxn[1][j * 128:(j + 1) * 128, :], B_sl[s][:])

            if g4 == 0:
                @block.scalar
                def _(scalar):
                    scalar.wait_ge(lw.h, lw_wq)
                    for e in range(ND):
                        i = scalar.activation(C_wh[:, e * D:(e + 1) * D],
                                              C_wst[:, e * D:(e + 1) * D], AF.Copy)
                        aev.inc(i)

                @block.vector
                def _(vector):
                    ab = aev.n - ND
                    for e in range(ND):
                        vector.wait_ge(aev.h, ab + e + 1)
                        i = vector.tensor_tensor(C_wl[:, e * D:(e + 1) * D],
                                                 C_wst[:, e * D:(e + 1) * D],
                                                 C_wh[:, e * D:(e + 1) * D], ALU.subtract)
                        lev.inc(i)

        def xnt_load(sync, tc, slot):
            sem = lx[slot]
            sync.wait_ge(sxg[tc].h, 128)
            for g in range(ND):
                rd(sync, sem, B_xth[slot][:, g * 512:(g + 1) * 512],
                   hxn[0][tc * 512:(tc + 1) * 512, g * 128:(g + 1) * 128], transpose=True)
                rd(sync, sem, B_xtl[slot][:, g * 512:(g + 1) * 512],
                   hxn[1][tc * 512:(tc + 1) * 512, g * 128:(g + 1) * 128], transpose=True)
            return sem.n

        def mm24(tensor, psum_ap, lhsT, rhsT, lslc, rslc):
            n = 0
            for g in range(ND):
                for (a, b) in PAIRS:
                    i = tensor.matmul(psum_ap, lhsT[a][lslc(g)], rhsT[b][rslc(g)],
                                      start=(n == 0), stop=(n == 23))
                    n += 1
            return i

        xnt_ld = {}
        mmtc = {}
        sq_tc = {}
        ve_red = {}
        red_plan = {1: range(0, 4), 2: range(4, 12), 3: range(12, 16)}


        WQ_LB = 0
        lo_val = {}
        wk_state = {}

        for tc in range(4):
            @block.sync
            def _(sync, tc=tc):
                if tc + 1 < 4:
                    if tc >= 1:
                        sync.wait_ge(mm.h, mmtc[tc - 1])
                    xnt_ld[("a", tc + 1)] = xnt_load(sync, tc + 1, (tc + 1) % 2)
                elif tc == 3:
                    sync.wait_ge(mm.h, mmtc[2])
                    xnt_ld[("b", 0)] = xnt_load(sync, 0, 0)

            @block.gpsimd
            def _(gpsimd, tc=tc):
                if tc == 1:
                    gpsimd.wait_ge(aev.h, WQ_LB + ND)
                    gpsimd.wait_ge(lev.h, WQ_LB + ND)
                    rd(gpsimd, lw, C_wst[:], wk_h)
                    wk_state["lw"] = lw.n

            lb_tc = lev.n

            @block.tensor
            def _(tensor, tc=tc, lb_tc=lb_tc):
                if tc == 0:
                    tensor.wait_ge(ve.h, xtb_evac[(3, 1)])
                else:
                    tensor.wait_ge(lx[tc % 2].h, xnt_ld[("a", tc)])
                slot = tc % 2
                for e in range(ND):
                    k = tc * ND + e
                    if tc == 0:
                        tensor.wait_ge(lev.h, WQ_LB + e + 1)
                    if k >= 4:
                        if e >= 4:
                            tensor.wait_ge(lev.h, lb_tc + (e - 4) + 1)
                        else:
                            tensor.wait_ge(lev.h, lo_val[k - 4])
                    i = mm24(tensor, P[k % 4][:, 0:512], (C_wh, C_wl),
                             (B_xth[slot], B_xtl[slot]),
                             lambda g, e=e: (slice(None), slice(e * D + g * 128, e * D + (g + 1) * 128)),
                             lambda g: (slice(None), slice(g * 512, (g + 1) * 512)))
                    mm.inc(i)
                mmtc[tc] = mm.n

            @block.scalar
            def _(scalar, tc=tc):
                base = mmtc[tc] - ND
                if tc >= 1:
                    scalar.wait_ge(sq.h, sq_tc[tc - 1])
                for e in range(ND):
                    k = tc * ND + e
                    scalar.wait_ge(mm.h, base + e + 1)
                    i = scalar.activation(B_qsh[:, e * 512:(e + 1) * 512], P[k % 4][:, 0:512],
                                          AF.Identity, bias=M_bq32[:, e:e + 1])
                    aev.inc(i)
            ab_a = aev.n - ND

            if tc == 3:
                @block.scalar
                def _(scalar, tc=tc):
                    scalar.wait_ge(lw.h, wk_state["lw"])
                    base = mmtc[3] - ND
                    for e in range(ND):
                        scalar.wait_ge(mm.h, base + e + 1)
                        i = scalar.activation(C_wh[:, e * D:(e + 1) * D],
                                              C_wst[:, e * D:(e + 1) * D], AF.Copy)
                        aev.inc(i)
                wk_state["ab"] = aev.n - ND

            @block.vector
            def _(vector, tc=tc, ab_a=ab_a):
                if tc >= 1:
                    vector.wait_ge(sq.h, sq_tc[tc - 1])
                for e in range(ND):
                    k = tc * ND + e
                    vector.wait_ge(aev.h, ab_a + e + 1)
                    i = vector.scalar_tensor_tensor(B_qsl[:, e * 512:(e + 1) * 512],
                                                    P[k % 4][:, 0:512], M_bq32[:, e:e + 1],
                                                    B_qsh[:, e * 512:(e + 1) * 512],
                                                    ALU.add, ALU.subtract)
                    lev.inc(i)
                    lo_val[k] = lev.n
            lev_a_tc = lev.n

            if tc == 3:
                @block.vector
                def _(vector):
                    for e in range(ND):
                        vector.wait_ge(aev.h, wk_state["ab"] + e + 1)
                        i = vector.tensor_tensor(C_wl[:, e * D:(e + 1) * D],
                                                 C_wst[:, e * D:(e + 1) * D],
                                                 C_wh[:, e * D:(e + 1) * D], ALU.subtract)
                        lev.inc(i)
                wk_state["lb"] = lev.n - ND

            if tc in red_plan:
                @block.vector
                def _(vector, tc=tc):
                    for t in red_plan[tc]:
                        i = vector.reduce_sum(M_sum2[:, t:t + 1], A_x[:, t * D:(t + 1) * D],
                                              axis=AX.X)
                        ve.inc(i)
                        ve_red[t] = ve.n

            @block.scalar
            def _(scalar, tc=tc, lev_a_tc=lev_a_tc):
                scalar.wait_ge(lev.h, lev_a_tc)
                qsh3 = B_qsh.rearrange("p (g w) -> p g w", g=ND)
                qsl3 = B_qsl.rearrange("p (g w) -> p g w", g=ND)
                for jj in range(4):
                    j = tc * 4 + jj
                    wr(scalar, sq, hqT[j, 0].rearrange("p (g u) -> p g u", g=ND),
                       qsh3[:, :, jj * 128:(jj + 1) * 128])
                    wr(scalar, sq, hqT[j, 1].rearrange("p (g u) -> p g u", g=ND),
                       qsl3[:, :, jj * 128:(jj + 1) * 128])
                sq_tc[tc] = sq.n

        mm_p1a = mmtc[3]
        P1A_HI_END = ab_a + ND
        P1A_LO_END = lo_val[31]
        WK_LB = wk_state["lb"]

        @block.scalar
        def _(scalar):
            scalar.wait_ge(ve.h, ve_red[15])
            ac.inc(scalar.activation(M_mu2[:], M_sum2[:], AF.Copy, scale=1.0 / 1024.0))
        ac_mu2 = ac.n

        @block.vector
        def _(vector):
            vector.wait_ge(ac.h, ac_mu2)
            vch(vector, vector.tensor_tensor(M_msq2[:], M_mu2[:], M_mu2[:], ALU.mult))
            vch(vector, vector.tensor_scalar(M_v1[:], M_v1[:], EPS, None, ALU.subtract))
            vch(vector, vector.tensor_tensor(M_v1[:], M_v1[:], M_r1[:], ALU.mult))
            vch(vector, vector.tensor_tensor(M_v1[:], M_v1[:], M_r1[:], ALU.mult))
            vch(vector, vector.tensor_tensor(M_v1[:], M_v1[:], M_msq2[:], ALU.subtract))
            ve.inc(vector.tensor_scalar_add(M_v1[:], M_v1[:], EPS))
        ve_v2 = ve.n

        @block.scalar
        def _(scalar):
            scalar.wait_ge(ve.h, ve_v2)
            ac.inc(scalar.activation(M_vv2[:], M_v1[:], AF.Sqrt))
        ac_sq2c = ac.n

        @block.vector
        def _(vector):
            vector.wait_ge(ac.h, ac_sq2c)
            ve.inc(vector.reciprocal(M_r2[:], M_vv2[:]))
        ve_r2 = ve.n
        ac_r2 = None

        mmtc_b = {}
        wv_state = {}

        for tc in range(4):
            @block.sync
            def _(sync, tc=tc):
                if tc + 1 < 4:
                    sync.wait_ge(mm.h, mmtc_b[tc - 1] if tc >= 1 else mm_p1a)
                    xnt_ld[("b", tc + 1)] = xnt_load(sync, tc + 1, (tc + 1) % 2)
                elif tc == 3:
                    sync.wait_ge(mm.h, mmtc_b[2])
                    xnt_ld[("c", 0)] = xnt_load(sync, 0, 0)

            @block.gpsimd
            def _(gpsimd, tc=tc):
                if tc == 0:
                    gpsimd.wait_ge(aev.h, wk_state["ab"] + ND)
                    gpsimd.wait_ge(lev.h, WK_LB + ND + 1)
                    rd(gpsimd, lw, C_wst[:, 0:2 * 4096], wv_h)
                    wv_state["lw"] = lw.n

            lb_tc = lev.n

            @block.tensor
            def _(tensor, tc=tc, lb_tc=lb_tc):
                tensor.wait_ge(lx[tc % 2].h, xnt_ld[("b", tc)])
                slot = tc % 2
                for e in range(ND):
                    k = tc * ND + e
                    if tc == 0:
                        tensor.wait_ge(lev.h, WK_LB + e + 1)
                    if k < 4:
                        tensor.wait_ge(lev.h, P1A_LO_END)
                        tensor.wait_ge(aev.h, P1A_HI_END)
                    else:
                        if e >= 4:
                            tensor.wait_ge(lev.h, lb_tc + (e - 4) + 1)
                        else:
                            tensor.wait_ge(lev.h, lo_val[32 + k - 4])
                    i = mm24(tensor, P[k % 4][:, 0:512], (C_wh, C_wl),
                             (B_xth[slot], B_xtl[slot]),
                             lambda g, e=e: (slice(None), slice(e * D + g * 128, e * D + (g + 1) * 128)),
                             lambda g: (slice(None), slice(g * 512, (g + 1) * 512)))
                    mm.inc(i)
                mmtc_b[tc] = mm.n

            @block.scalar
            def _(scalar, tc=tc):
                base = mmtc_b[tc] - ND
                if tc == 0:
                    scalar.wait_ge(ve.h, ve_red[7])
                for e in range(ND):
                    k = tc * ND + e
                    scalar.wait_ge(mm.h, base + e + 1)
                    dst = slice(e * T + tc * 512, e * T + (tc + 1) * 512)
                    i = scalar.activation(A_kth[:, dst], P[k % 4][:, 0:512],
                                          AF.Identity, bias=M_bkT[:, e:e + 1])
                    aev.inc(i)
            ab_b = aev.n - ND

            if tc == 1:
                @block.scalar
                def _(scalar):
                    scalar.wait_ge(lw.h, wv_state["lw"])
                    scalar.wait_ge(sq.h, sq_tc[3])
                    for h2 in range(2):
                        i = scalar.activation(B_wvh[:, h2 * 4096:(h2 + 1) * 4096],
                                              C_wst[:, h2 * 4096:(h2 + 1) * 4096], AF.Copy)
                        aev.inc(i)
                wv_state["ab"] = aev.n - 2

            @block.vector
            def _(vector, tc=tc, ab_b=ab_b):
                for e in range(ND):
                    k = tc * ND + e
                    vector.wait_ge(aev.h, ab_b + e + 1)
                    dst = slice(e * T + tc * 512, e * T + (tc + 1) * 512)
                    i = vector.scalar_tensor_tensor(A_ktl[:, dst], P[k % 4][:, 0:512],
                                                    M_bkT[:, e:e + 1], A_kth[:, dst],
                                                    ALU.add, ALU.subtract)
                    lev.inc(i)
                    lo_val[32 + k] = lev.n
            if tc == 1:
                @block.vector
                def _(vector):
                    vector.wait_ge(sxg[3].h, 128)
                    vector.wait_ge(sq.h, sq_tc[3])
                    for h2 in range(2):
                        vector.wait_ge(aev.h, wv_state["ab"] + h2 + 1)
                        i = vector.tensor_tensor(B_wvl[:, h2 * 4096:(h2 + 1) * 4096],
                                                 C_wst[:, h2 * 4096:(h2 + 1) * 4096],
                                                 B_wvh[:, h2 * 4096:(h2 + 1) * 4096], ALU.subtract)
                        lev.inc(i)
                wv_state["lb"] = lev.n - 2

        mm_p1b = mmtc_b[3]
        P1B_HI_END = ab_b + ND
        P1B_LO_END = lev.n
        WV_LB = wv_state["lb"]

        mmv = {}

        for tc in range(4):
            @block.sync
            def _(sync, tc=tc):
                if tc + 1 < 4:
                    sync.wait_ge(mm.h, mmv[tc - 1] if tc >= 1 else mm_p1b)
                    xnt_ld[("c", tc + 1)] = xnt_load(sync, tc + 1, (tc + 1) % 2)

            lb_tc = lev.n

            @block.tensor
            def _(tensor, tc=tc, lb_tc=lb_tc):
                tensor.wait_ge(lx[tc % 2].h, xnt_ld[("c", tc)])
                slot = tc % 2
                for jj in range(4):
                    for h2 in range(2):
                        k = (tc * 4 + jj) * 2 + h2
                        m = jj * 2 + h2
                        if k == 0:
                            tensor.wait_ge(lev.h, WV_LB + 1)
                        elif k == 1:
                            tensor.wait_ge(lev.h, WV_LB + 2)
                        if k < 4:
                            tensor.wait_ge(lev.h, P1B_LO_END)
                            tensor.wait_ge(aev.h, P1B_HI_END)
                        else:
                            if m >= 4:
                                tensor.wait_ge(lev.h, lb_tc + (m - 4) + 1)
                            else:
                                tensor.wait_ge(lev.h, lo_val[64 + k - 4])
                        i = mm24(tensor, P[k % 4][:, 0:512], (B_xth[slot], B_xtl[slot]),
                                 (B_wvh, B_wvl),
                                 lambda g, jj=jj: (slice(None), slice(g * 512 + jj * 128, g * 512 + (jj + 1) * 128)),
                                 lambda g, h2=h2: (slice(None), slice(h2 * 4096 + g * 512, h2 * 4096 + (g + 1) * 512)))
                        mm.inc(i)
                mmv[tc] = mm.n

            @block.scalar
            def _(scalar, tc=tc):
                base = mmv[tc] - 8
                for jj in range(4):
                    for h2 in range(2):
                        k = (tc * 4 + jj) * 2 + h2
                        m = jj * 2 + h2
                        scalar.wait_ge(mm.h, base + m + 1)
                        if k == 0:
                            scalar.wait_ge(lev.h, WV_LB + 2)
                        dst = slice((tc * 4 + jj) * D + h2 * 512, (tc * 4 + jj) * D + (h2 + 1) * 512)
                        i = scalar.activation(C_vh[:, dst], P[k % 4][:, 0:512], AF.Copy)
                        aev.inc(i)
            ab_c = aev.n - 8

            @block.vector
            def _(vector, tc=tc, ab_c=ab_c):
                for jj in range(4):
                    for h2 in range(2):
                        k = (tc * 4 + jj) * 2 + h2
                        m = jj * 2 + h2
                        vector.wait_ge(aev.h, ab_c + m + 1)
                        dst = slice((tc * 4 + jj) * D + h2 * 512, (tc * 4 + jj) * D + (h2 + 1) * 512)
                        i = vector.tensor_tensor(C_vl[:, dst], P[k % 4][:, 0:512], C_vh[:, dst],
                                                 ALU.subtract)
                        lev.inc(i)
                        lo_val[64 + k] = lev.n

        mm_p1c = mmv[3]
        AEV_P1 = aev.n
        LEV_P1 = lev.n

        qt_ld = {}
        mm_sc = {}
        ac_scev = {}
        ve_msk = {}
        ac_exp = {}
        ve_pspl = {}
        ve_ptev = {}
        mm_pv = {}
        ve_avev = {}
        sa_t = {}

        lxp2 = [Sem(nc, es, f"lxp2{i}") for i in range(2)]
        xp_ld = {}
        xp_consumed = {0: None, 1: None}
        ve_xp = {}
        gp_a = {}
        ve_b = {}
        gp_c = {}
        ac_sq2u = {}
        ve_h2 = {}
        p0b_state = {"ve_r2": None}

        def emit_pass2_unit(u):
            t = u
            slot = u % 2

            @block.sync
            def _(sync, u=u, t=t, slot=slot):
                if xp_consumed[slot] is not None:
                    s_, v_ = xp_consumed[slot]
                    sync.wait_ge(s_.h, v_)
                rd(sync, lxp2[slot], B_xph[slot][:], hxn[0][t * 128:(t + 1) * 128, :])
                xp_ld[16 + u] = rd(sync, lxp2[slot], B_xpl[slot][:], hxn[1][t * 128:(t + 1) * 128, :])

            @block.vector
            def _(vector, u=u, t=t, slot=slot):
                vector.wait_ge(lxp2[slot].h, xp_ld[16 + u])
                if u == 0:
                    vector.wait_ge(ve.h, p0b_state["ve_r2"])
                if u >= 1:
                    vector.wait_ge(ve.h, ve_h2[u - 1])
                if u >= 2:
                    vector.wait_ge(shh[u % 2].h, 32 * ((u - 2) // 2 + 1))
                vch(vector, vector.tensor_add(B_tf[:], B_xph[slot][:], B_xpl[slot][:]))
                vch(vector, vector.tensor_scalar(B_tf[:], B_tf[:], M_mu2[:, t:t + 1],
                                                 M_r2[:, t:t + 1], ALU.subtract, ALU.mult))
                vch(vector, vector.tensor_tensor(B_tf[:], B_tf[:], M_gam[:], ALU.mult))
                vch(vector, vector.tensor_tensor(B_tf[:], B_tf[:], M_bet[:], ALU.add))
                vch(vector, vector.tensor_copy(B_sh[slot][:], B_tf[:]))
                i = vector.tensor_tensor(B_sl[slot][:], B_tf[:], B_sh[slot][:], ALU.subtract)
                ve.inc(i)
                ve_h2[u] = ve.n
                xp_consumed[slot] = (ve, ve.n)

            @block.scalar
            def _(scalar, u=u, t=t, slot=slot):
                scalar.wait_ge(ve.h, ve_h2[u])
                wr(scalar, shh[slot], hh[0][t * 128:(t + 1) * 128, :], B_sh[slot][:])
                wr(scalar, shh[slot], hh[1][t * 128:(t + 1) * 128, :], B_sl[slot][:])

        p0b_state["ve_r2"] = ve_r2

        def emit_p0b(j):
            emit_pass2_unit(j)

        mm_tr = {}
        nch_of = lambda t: ((t + 1) * 128 + 511) // 512
        w_of = lambda t, c: min(512, (t + 1) * 128 - c * 512)

        @block.sync
        def _(sync):
            sync.wait_ge(sq.h, sq_tc[3])
            sync.wait_ge(mm.h, mm_p1c)
            rd(sync, lq[0], B_qth[0][:], hqT[0, 0])
            qt_ld[0] = rd(sync, lq[0], B_qtl[0][:], hqT[0, 1])

        def emit_exp(t, ch):
            @block.scalar
            def _(scalar, t=t, ch=ch):
                scalar.wait_ge(ve.h, ve_msk[t])
                if ch >= 2:
                    scalar.wait_ge(ve.h, ve_pspl[(t, ch - 2)])
                w = w_of(t, ch)
                i = scalar.activation(B_pf[ch % 2][:, 0:w], B_sc[:, ch * 512:ch * 512 + w],
                                      AF.Exp, bias=M_negm[:, t:t + 1],
                                      accum_out=M_zc[:, ch:ch + 1])
                ac.inc(i)
                ac_exp[(t, ch)] = ac.n

        def emit_split(t, ch):
            @block.vector
            def _(vector, t=t, ch=ch):
                vector.wait_ge(ac.h, ac_exp[(t, ch)])
                if ch == 0 and t >= 1:
                    vector.wait_ge(mm.h, mm_tr[t - 1])
                w = w_of(t, ch)
                cs = slice(ch * 512, ch * 512 + w)
                vch(vector, vector.tensor_copy(B_ph[:, cs], B_pf[ch % 2][:, 0:w]))
                i = vector.tensor_tensor(B_pl[:, cs], B_pf[ch % 2][:, 0:w], B_ph[:, cs],
                                         ALU.subtract)
                ve.inc(i)
                ve_pspl[(t, ch)] = ve.n

        def emit_evac(j, ch, after_exp=None):
            @block.scalar
            def _(scalar, j=j, ch=ch, after_exp=after_exp):
                scalar.wait_ge(mm.h, mm_sc[(j, ch)])
                if after_exp is not None:
                    scalar.wait_ge(ac.h, after_exp)
                w = w_of(j, ch)
                i = scalar.activation(B_sc[:, ch * 512:ch * 512 + w], P[ch % 2][:, 0:w], AF.Copy,
                                      scale=1.0 / 32.0)
                ac.inc(i)
                ac_scev[(j, ch)] = ac.n

        def emit_z16(t):
            @block.vector
            def _(vector, t=t):
                i = vector.reduce_sum(M_z16[:, t:t + 1], M_zc[:, 0:nch_of(t)], axis=AX.X)
                ve.inc(i)

        def emit_tr(t):
            nblk = t + 1
            for half in range(2):
                srcb = B_ph if half == 0 else B_pl
                dstb = B_pth if half == 0 else B_ptl
                for g0 in range(0, nblk, 8):
                    gn = min(8, nblk - g0)

                    @block.tensor
                    def _(tensor, t=t, half=half, g0=g0, gn=gn, srcb=srcb, nblk=nblk):
                        tensor.wait_ge(ve.h, ve_pspl[(t, ((g0 + gn) * 128 - 1) // 512)])
                        if g0 >= 8:
                            prevk = (t, half, g0 - 8)
                        elif half == 1:
                            prevk = (t, 0, ((nblk - 1) // 8) * 8)
                        elif t >= 1:
                            prevk = (t - 1, 1, ((t - 1) // 8) * 8)
                        else:
                            prevk = None
                        if prevk is not None and prevk in ve_ptev:
                            tensor.wait_ge(ve.h, ve_ptev[prevk])
                        for b in range(g0, g0 + gn):
                            i = tensor.transpose(PTp[:, (b - g0) * 128:(b - g0 + 1) * 128],
                                                 srcb[:, b * 128:(b + 1) * 128], M_eyeb[:])
                        mm.inc(i)

                    @block.vector
                    def _(vector, t=t, half=half, g0=g0, gn=gn, dstb=dstb):
                        vector.wait_ge(mm.h, mm.n)
                        i = vector.tensor_copy(dstb[:, g0 * 128:(g0 + gn) * 128], PTp[:, 0:gn * 128])
                        ve.inc(i)
                        ve_ptev[(t, half, g0)] = ve.n
            mm_tr[t] = mm.n

        def emit_pv(t):
            nblk = t + 1

            @block.tensor
            def _(tensor, t=t, nblk=nblk):
                tensor.wait_ge(ve.h, ve_ptev[(t, 1, ((nblk - 1) // 8) * 8)])
                if t >= 1:
                    tensor.wait_ge(ve.h, ve_avev[t - 1])
                for h2 in range(2):
                    n = 0
                    for b in range(nblk):
                        for (a, c) in PAIRS:
                            i = tensor.matmul(P[2 + h2][:, 0:512],
                                              (B_pth, B_ptl)[a][:, b * 128:(b + 1) * 128],
                                              (C_vh, C_vl)[c][:, b * D + h2 * 512:b * D + (h2 + 1) * 512],
                                              start=(n == 0), stop=(n == 3 * nblk - 1))
                            n += 1
                    mm.inc(i)
                    mm_pv[(t, h2)] = mm.n

        def emit_avev(t):
            @block.vector
            def _(vector, t=t):
                vector.wait_ge(mm.h, mm_pv[(t, 1)])
                if t >= 2:
                    vector.wait_ge(sa[t % 2].h, 16 * ((t - 2) // 2 + 1))
                s = t % 2
                vector.tensor_copy(B_ast[s][:, 0:512], P[2][:, 0:512])
                i = vector.tensor_copy(B_ast[s][:, 512:1024], P[3][:, 0:512])
                ve.inc(i)
                ve_avev[t] = ve.n

        def emit_spill(t):
            @block.scalar
            def _(scalar, t=t):
                scalar.wait_ge(ve.h, ve_avev[t])
                wr(scalar, sa[t % 2], hattn[t], B_ast[t % 2][:])
                sa_t[t] = (sa[0].n, sa[1].n)

        for j in range(NT):
            nch = nch_of(j)
            pnch = nch_of(j - 1) if j >= 1 else 0

            @block.sync
            def _(sync, j=j):
                if j + 1 < NT:
                    if j >= 1:
                        sync.wait_ge(mm.h, mm_sc[(j - 1, nch_of(j - 1) - 1)])
                    s1_ = (j + 1) % 2
                    rd(sync, lq[s1_], B_qth[s1_][:], hqT[j + 1, 0])
                    qt_ld[j + 1] = rd(sync, lq[s1_], B_qtl[s1_][:], hqT[j + 1, 1])

            acb = ac.n
            for ch in range(nch):
                @block.tensor
                def _(tensor, j=j, ch=ch, nch=nch, acb=acb, pnch=pnch):
                    if j == 0 and ch == 0:
                        tensor.wait_ge(lev.h, LEV_P1)
                        tensor.wait_ge(aev.h, AEV_P1)
                    if ch == 0:
                        tensor.wait_ge(lq[j % 2].h, qt_ld[j])
                        if j >= 1:
                            tensor.wait_ge(ac.h, ac_scev[(j - 1, nch_of(j - 1) - 1)])
                    if ch >= 2:
                        tensor.wait_ge(ac.h, acb + pnch + (ch - 2) + 1)
                    slot = j % 2
                    w = w_of(j, ch)
                    i = mm24(tensor, P[ch % 2][:, 0:w], (B_qth[slot], B_qtl[slot]),
                             (A_kth, A_ktl),
                             lambda g: (slice(None), slice(g * 128, (g + 1) * 128)),
                             lambda g, ch=ch, w=w: (slice(None), slice(g * T + ch * 512, g * T + ch * 512 + w)))
                    mm.inc(i)
                    mm_sc[(j, ch)] = mm.n
            for ch in range(pnch):
                emit_exp(j - 1, ch)
                emit_split(j - 1, ch)
            for ch in range(nch):
                emit_evac(j, ch, after_exp=ac_exp[(j - 1, ch)] if ch < pnch else None)
            if j >= 1:
                emit_z16(j - 1)

            @block.vector
            def _(vector, j=j, nch=nch):
                vector.wait_ge(ac.h, ac_scev[(j, nch - 1)])
                dg = slice(j * 128, j * 128 + 128)
                vch(vector, vector.tensor_tensor(B_sc[:, dg], B_sc[:, dg], M_mask[:], ALU.add))
                i = vector.reduce_max(M_negm[:, j:j + 1], B_sc[:, 0:(j + 1) * 128],
                                      axis=AX.X, negate=True)
                ve.inc(i)
                ve_msk[j] = ve.n

            if j >= 1:
                emit_tr(j - 1)
            if j >= 2:
                emit_p0b(j - 2)

            if j >= 1:
                emit_pv(j - 1)
                emit_avev(j - 1)
                emit_spill(j - 1)

        fl = NT - 1
        for ch in range(nch_of(fl)):
            emit_exp(fl, ch)
            emit_split(fl, ch)
        emit_z16(fl)
        emit_tr(fl)
        emit_p0b(fl - 1)
        emit_p0b(fl)
        emit_pv(fl)
        emit_avev(fl)
        emit_spill(fl)

        shh_total = (shh[0].n, shh[1].n)

        @block.gpsimd
        def _(gpsimd):
            gpsimd.wait_ge(ve.h, ve.n)
            gp.inc(gpsimd.tensor_scalar_mul(M_m16[:], M_negm[:], -1.0))
        gp_m16 = gp.n

        @block.vector
        def _(vector):
            vector.wait_ge(gp.h, gp_m16)
            ve.inc(vector.reduce_max(M_col[:], M_m16[:], axis=AX.X))
        ve_mcol = ve.n

        @block.gpsimd
        def _(gpsimd):
            gpsimd.wait_ge(ve.h, ve_mcol)
            gp.inc(gpsimd.tensor_reduce(M_m11[:1, 0:1], M_col[:], AX.C, ALU.max))
        gp_m11 = gp.n

        @block.tensor
        def _(tensor):
            tensor.wait_ge(gp.h, gp_m11)
            mm.inc(tensor.matmul(PS[:, 0:1], M_rows[:], M_m11[:1, 0:1], start=True, stop=True))
        mm_negmg = mm.n

        @block.vector
        def _(vector):
            vector.wait_ge(mm.h, mm_negmg)
            ve.inc(vector.tensor_copy(M_negmg[:], PS[:, 0:1]))
        ve_negmg = ve.n

        @block.scalar
        def _(scalar):
            scalar.wait_ge(ve.h, ve_negmg)
            ac.inc(scalar.activation(M_e16[:], M_m16[:], AF.Exp, bias=M_negmg[:]))
        ac_e16 = ac.n

        @block.gpsimd
        def _(gpsimd):
            gpsimd.wait_ge(ac.h, ac_e16)
            gp.inc(gpsimd.tensor_mul(M_ss[:], M_e16[:], M_z16[:]))
        gp_zsc = gp.n

        @block.vector
        def _(vector):
            vector.wait_ge(gp.h, gp_zsc)
            ve.inc(vector.reduce_sum(M_col[:], M_ss[:], axis=AX.X))
        ve_zcol = ve.n

        @block.gpsimd
        def _(gpsimd):
            gpsimd.wait_ge(ve.h, ve_zcol)
            gp.inc(gpsimd.tensor_reduce(M_m11[:1, 1:2], M_col[:], AX.C, ALU.add))
        gp_z11 = gp.n

        @block.tensor
        def _(tensor):
            tensor.wait_ge(gp.h, gp_z11)
            mm.inc(tensor.matmul(PS[:, 1:2], M_pones[:], M_m11[:1, 1:2], start=True, stop=True))
        mm_zbb = mm.n

        @block.scalar
        def _(scalar):
            scalar.wait_ge(mm.h, mm_zbb)
            ac.inc(scalar.activation(M_Zb[:], PS[:, 1:2], AF.Copy))
        ac_zbb = ac.n

        @block.vector
        def _(vector):
            vector.wait_ge(ac.h, ac_zbb)
            ve.inc(vector.reciprocal(M_rZb[:], M_Zb[:]))
        ve_rzb = ve.n

        @block.scalar
        def _(scalar):
            scalar.wait_ge(ve.h, ve_rzb)
            ac.inc(scalar.activation(M_c16[:], M_e16[:], AF.Copy, scale=M_rZb[:]))
        ac_c16 = ac.n

        @block.gpsimd
        def _(gpsimd):
            gpsimd.wait_ge(ac.h, ac_c16)
            gp.inc(gpsimd.tensor_mul(M_g16[:], M_c16[:], M_z16[:]))
        gp_g16 = gp.n

        dq_ld = {}
        ve_dq = {}
        ve_dqblk = {}

        ht_ld = {}
        mm_y = {}
        ac_y = {}
        ve_y = {}
        sy_t = {}
        mm_tc_end = {}

        @block.sync
        def _(sync):
            sync.wait_ge(shh[0].h, shh_total[0])
            sync.wait_ge(shh[1].h, shh_total[1])
            sync.wait_ge(ve.h, ve_pspl[(NT - 1, 3)])
            sync.wait_ge(mm.h, mm_tr[NT - 1])
            sync.wait_ge(ac.h, ac_exp[(NT - 1, 3)])
            for g in range(ND):
                rd(sync, lh[0], B_hth[0][:, g * 512:(g + 1) * 512],
                   hh[0][0:512, g * 128:(g + 1) * 128], transpose=True)
                rd(sync, lh[0], B_htl[0][:, g * 512:(g + 1) * 512],
                   hh[1][0:512, g * 128:(g + 1) * 128], transpose=True)
        ht_ld[0] = lh[0].n


        ve_stt1 = {}
        ac_cp1 = {}
        PD1 = [P[4], P[5], PS]

        def emit_dq1_unit(fb, g):
            k = fb * ND + g

            @block.sync
            def _(sync, g=g, fb=fb, k=k):
                if k == 0:
                    sync.wait_ge(lev.h, LEV_P1)
                    sync.wait_ge(shh[0].h, shh_total[0])
                    sync.wait_ge(shh[1].h, shh_total[1])
                if k >= 3:
                    sync.wait_ge(ve.h, ve_stt1[k - 3])
                rd(sync, ldq[k % 3], B_dqq[k % 3][:], q1_h[g, fb])
                dq_ld[k] = rd(sync, ldq[k % 3], B_dqs[k % 3][:], s1_h[g, fb])

            @block.vector
            def _(vector, g=g, fb=fb, k=k):
                vector.wait_ge(ldq[k % 3].h, dq_ld[k])
                vch(vector, vector.scalar_tensor_tensor(PD1[k % 3][:, 0:512], B_dqq[k % 3][:],
                                                        -128.0, B_dqs[k % 3][:],
                                                        ALU.add, ALU.mult))
                ve_stt1[k] = ve.n
                dst = slice(g * FF + fb * 512, g * FF + (fb + 1) * 512)
                vch(vector, vector.tensor_copy(A_w1h[:, dst], PD1[k % 3][:, 0:512]))
                i = vector.tensor_tensor(C_w1l[:, dst], PD1[k % 3][:, 0:512], A_w1h[:, dst],
                                         ALU.subtract)
                ve.inc(i)
                ve_dq[k] = ve.n


        def emit_fc1_unit(tc, fc):
            k = tc * NF + fc

            @block.tensor
            def _(tensor, tc=tc, fc=fc, k=k):
                if fc == 0:
                    tensor.wait_ge(lh[tc % 2].h, ht_ld[tc])
                if tc == 0:
                    tensor.wait_ge(ve.h, ve_dqblk[fc // 4])
                if k >= 2:
                    tensor.wait_ge(ac.h, ac_y[k - 2])
                slot = tc % 2
                i = mm24(tensor, P[k % 2][:, 0:512], (A_w1h, C_w1l),
                         (B_hth[slot], B_htl[slot]),
                         lambda g, fc=fc: (slice(None), slice(g * FF + fc * 128, g * FF + (fc + 1) * 128)),
                         lambda g: (slice(None), slice(g * 512, (g + 1) * 512)))
                mm.inc(i)
                mm_y[k] = mm.n

            @block.scalar
            def _(scalar, fc=fc, k=k):
                scalar.wait_ge(mm.h, mm_y[k])
                if k >= 2:
                    scalar.wait_ge(gp.h, ve_y[k - 2])
                i = scalar.activation(B_yf[k % 2][:], P[k % 2][:, 0:512], AF.Silu,
                                      bias=M_b1T[:, fc:fc + 1])
                ac.inc(i)
                ac_y[k] = ac.n

            @block.gpsimd
            def _(gpsimd, tc=tc, fc=fc, k=k):
                gpsimd.wait_ge(ac.h, ac_y[k])
                if k == 0:
                    gpsimd.wait_ge(sa[0].h, sa_t[NT - 1][0])
                    gpsimd.wait_ge(sa[1].h, sa_t[NT - 1][1])
                if fc % 8 == 0 and k >= 8:
                    tp, vp = sy_t[k - 1]
                    gpsimd.wait_ge(sy4[tp].h, vp)
                c0 = (fc % 8) * 256
                gpsimd.tensor_copy(B_yst[0][0][:, c0:c0 + 256], B_yf[k % 2][:, 0:256])
                i = gpsimd.tensor_copy(B_yst[1][0][:, c0:c0 + 256], B_yf[k % 2][:, 256:512])
                v_ = gp.inc(i)
                gpsimd.wait_ge(gp.h, v_)
                for s2b in range(2):
                    i = gpsimd.tensor_tensor(B_yst[s2b][1][:, c0:c0 + 256],
                                             B_yf[k % 2][:, s2b * 256:(s2b + 1) * 256],
                                             B_yst[s2b][0][:, c0:c0 + 256], ALU.subtract)
                gp.inc(i)
                ve_y[k] = gp.n

            if fc % 8 == 7:
                @block.scalar
                def _(scalar, tc=tc, fc=fc, k=k):
                    scalar.wait_ge(gp.h, ve_y[k])
                    fg = fc // 8
                    for s2b in range(2):
                        for half in range(2):
                            wr(scalar, sy4[tc],
                               hy[tc, s2b, half, :, fg * 8:(fg + 1) * 8],
                               B_yst[s2b][half].rearrange("p (f u) -> p f u", f=8))
                    sy_t[k] = (tc, sy4[tc].n)

        def emit_ht_prefetch(tc):
            @block.sync
            def _(sync, tc=tc):
                if tc >= 2:
                    sync.wait_ge(mm.h, mm_tc_end[tc - 2])
                s1_ = tc % 2
                for g in range(ND):
                    rd(sync, lh[s1_], B_hth[s1_][:, g * 512:(g + 1) * 512],
                       hh[0][tc * 512:(tc + 1) * 512, g * 128:(g + 1) * 128], transpose=True)
                    rd(sync, lh[s1_], B_htl[s1_][:, g * 512:(g + 1) * 512],
                       hh[1][tc * 512:(tc + 1) * 512, g * 128:(g + 1) * 128], transpose=True)
                ht_ld[tc] = lh[s1_].n

        for fb in range(8):
            for g in range(ND):
                emit_dq1_unit(fb, g)
            ve_dqblk[fb] = ve.n
            if fb == 1:
                emit_ht_prefetch(1)
            if fb >= 1:
                for fc in range(4 * (fb - 1), 4 * (fb - 1) + 4):
                    emit_fc1_unit(0, fc)
        for fc in range(28, 32):
            emit_fc1_unit(0, fc)
        mm_tc_end[0] = mm.n

        for tc in range(1, 4):
            if tc + 1 < 4:
                emit_ht_prefetch(tc + 1)
            for fc in range(NF):
                emit_fc1_unit(tc, fc)
            mm_tc_end[tc] = mm.n

        ve_w1 = ve.n

        dq2_ld = {}
        lybb = [0]
        ve_stt2 = {}
        ac_cp2 = {}
        ve_dq2 = {}

        yg_ld = {}
        yq_ld = {}
        xa_ld = {}
        mm_z = {}
        mm_zh = {}
        ve_cmb = {}
        ve_cmbh = {}
        so_t = {}

        def ygrp_load(sync, grp, chase=False):
            tc_, sub_ = grp // 2, grp % 2
            sync.wait_ge(sy4[tc_].h, 256)
            for q4 in range(4):
                if chase:
                    if q4 < 3:
                        sync.wait_ge(mmq.h, (grp - 1) * 3 + q4 + 1)
                    else:
                        sync.wait_ge(mm.h, mm_z[grp - 1])
                for half in range(2):
                    dstb = (B_ygh, B_ygl)[half]
                    rd(sync, lyq[q4],
                       dstb.rearrange("p (f u) -> p f u", f=NF)[:, q4 * 8:(q4 + 1) * 8, :],
                       hy[tc_, sub_, half, :, q4 * 8:(q4 + 1) * 8])
                yq_ld[(grp, q4)] = lyq[q4].n

        @block.sync
        def _(sync):
            sync.wait_ge(mm.h, mm_tc_end[2])
            sync.wait_ge(sy4[0].h, 256)
            for q4 in range(4):
                rd(sync, lyq[q4],
                   B_ygh.rearrange("p (f u) -> p f u", f=NF)[:, q4 * 8:(q4 + 1) * 8, :],
                   hy[0, 0, 0, :, q4 * 8:(q4 + 1) * 8])

        @block.sync
        def _(sync):
            sync.wait_ge(mm.h, mm_tc_end[3])
            sync.wait_ge(sa[0].h, sa_t[NT - 1][0])
            sync.wait_ge(sa[1].h, sa_t[NT - 1][1])
            for q4 in range(4):
                rd(sync, lyq[q4],
                   B_ygl.rearrange("p (f u) -> p f u", f=NF)[:, q4 * 8:(q4 + 1) * 8, :],
                   hy[0, 0, 1, :, q4 * 8:(q4 + 1) * 8])
                yq_ld[(0, q4)] = lyq[q4].n


        for fc in range(NF):
            @block.sync
            def _(sync, fc=fc):
                if fc == 0:
                    sync.wait_ge(mm.h, mm_tc_end[3])
                    sync.wait_ge(ve.h, ve.n)
                    sync.wait_ge(gp.h, gp.n)
                    sync.wait_ge(ac.h, ac_y[4 * NF - 1])
                    for tci in range(4):
                        sync.wait_ge(sy4[tci].h, 256)
                    rd(sync, ly, M_bvb[:], bvb_h)
                    lybb[0] = rd(sync, ly, M_b2b[:], b2b_h)
                if fc >= 2:
                    sync.wait_ge(ve.h, ve_dq2[fc - 2])
                rd(sync, ldq[fc % 2], B_dq2q[fc % 2][:], q2_h[fc * 128:(fc + 1) * 128, :])
                dq2_ld[fc] = rd(sync, ldq[fc % 2], B_dq2s[fc % 2][:], s2_h[fc])

            @block.vector
            def _(vector, fc=fc):
                vector.wait_ge(ldq[fc % 2].h, dq2_ld[fc])
                if fc >= 2:
                    vector.wait_ge(ac.h, ac_cp2[fc - 2])
                i = vector.scalar_tensor_tensor(B_dq2f[fc % 2][:], B_dq2q[fc % 2][:], -128.0,
                                                B_dq2s[fc % 2][:], ALU.add, ALU.mult)
                ve.inc(i)
                ve_stt2[fc] = ve.n

            @block.scalar
            def _(scalar, fc=fc):
                scalar.wait_ge(ve.h, ve_stt2[fc])
                dst = slice(fc * D, (fc + 1) * D)
                i = scalar.activation(A_w2h[:, dst], B_dq2f[fc % 2][:], AF.Copy)
                ac.inc(i)
                ac_cp2[fc] = ac.n

            @block.vector
            def _(vector, fc=fc):
                vector.wait_ge(ac.h, ac_cp2[fc])
                dst = slice(fc * D, (fc + 1) * D)
                i = vector.tensor_tensor(C_w2l[:, dst], B_dq2f[fc % 2][:], A_w2h[:, dst],
                                         ALU.subtract)
                ve.inc(i)
                ve_dq2[fc] = ve.n
        ve_w2 = ve.n

        for grp in range(8):
            tc, sub = grp // 2, grp % 2

            @block.sync
            def _(sync, grp=grp, tc=tc, sub=sub):
                if grp >= 1:
                    ygrp_load(sync, grp, chase=True)

            @block.sync
            def _(sync, grp=grp, tc=tc, sub=sub):
                if grp == 0:
                    sync.wait_ge(ve.h, ve_w2)
                else:
                    sync.wait_ge(ve.h, ve_cmb[grp - 1])
                for jj in range(2):
                    j = tc * 4 + sub * 2 + jj
                    rd(sync, lxa, B_xt2[jj][:], x_h[j * 128:(j + 1) * 128, :])
                    rd(sync, lxa, B_at2[jj][:], hattn[j])
                xa_ld[grp] = lxa.n

            for jj in range(2):
                hg = grp * 2 + jj
                pb = 2 * (hg % 3)

                @block.tensor
                def _(tensor, grp=grp, jj=jj, hg=hg, pb=pb):
                    if hg == 0:
                        tensor.wait_ge(ac.h, ac_y[4 * NF - 1])
                    elif hg >= 3:
                        tensor.wait_ge(ve.h, ve_cmbh[hg - 3])
                    n = 0
                    for fc in range(NF):
                        if jj == 0 and fc % 8 == 0:
                            tensor.wait_ge(lyq[fc // 8].h, yq_ld[(grp, fc // 8)])
                        if hg == 0:
                            tensor.wait_ge(ve.h, ve_dq2[fc])
                        for h2 in range(2):
                            for (a, c) in PAIRS:
                                i = tensor.matmul(
                                    P[pb + h2][:, 0:512],
                                    (B_ygh, B_ygl)[a][:, fc * 256 + jj * 128:fc * 256 + (jj + 1) * 128],
                                    (A_w2h, C_w2l)[c][:, fc * D + h2 * 512:fc * D + (h2 + 1) * 512],
                                    start=(fc == 0 and n % 3 == 0), stop=(fc == NF - 1 and n % 3 == 2))
                                n += 1
                        if jj == 1 and fc % 8 == 7 and fc != NF - 1:
                            mmq.inc(i)
                    mm.inc(i)
                    mm_zh[hg] = mm.n

                @block.vector
                def _(vector, grp=grp, tc=tc, sub=sub, jj=jj, hg=hg, pb=pb):
                    vector.wait_ge(mm.h, mm_zh[hg])
                    vector.wait_ge(lxa.h, xa_ld[grp])
                    if hg == 0:
                        vector.wait_ge(gp.h, gp_g16)
                        vector.wait_ge(ly.h, lybb[0])
                    if hg >= 2:
                        vector.wait_ge(so.h, so_t.get((hg - 2) // 2, 0))
                    j = tc * 4 + sub * 2 + jj
                    for h2 in range(2):
                        hs = slice(h2 * 512, (h2 + 1) * 512)
                        t0 = B_t0[h2]
                        vector.tensor_scalar(t0[:], B_xt2[jj][:, hs], M_mu1[:, j:j + 1],
                                             M_r1[:, j:j + 1], ALU.subtract, ALU.mult)
                        vch(vector, vector.tensor_tensor(B_zb[h2][:], P[pb + h2][:, 0:512],
                                                         M_b2b[:, hs], ALU.add))
                        vch(vector, vector.tensor_tensor(t0[:], t0[:], B_xt2[jj][:, hs], ALU.add))
                        vch(vector, vector.scalar_tensor_tensor(t0[:], B_at2[jj][:, hs],
                                                                M_c16[:, j:j + 1], t0[:],
                                                                ALU.mult, ALU.add))
                        vch(vector, vector.scalar_tensor_tensor(t0[:], M_bvb[:, hs],
                                                                M_g16[:, j:j + 1], t0[:],
                                                                ALU.mult, ALU.add))
                        i = vector.scalar_tensor_tensor(B_ost[jj][:, hs], B_zb[h2][:], 0.5,
                                                        t0[:], ALU.mult, ALU.add)
                    ve.inc(i)
                    ve_cmbh[hg] = ve.n
            ve_cmb[grp] = ve.n
            mm_z[grp] = mm.n

            @block.scalar
            def _(scalar, grp=grp, tc=tc, sub=sub):
                scalar.wait_ge(ve.h, ve_cmb[grp])
                for jj in range(2):
                    j = tc * 4 + sub * 2 + jj
                    wr(scalar, so, out_h[j * 128:(j + 1) * 128, :], B_ost[jj][:])
                so_t[grp] = so.n

        @block.scalar
        def _(scalar):
            scalar.wait_ge(so.h, so.n)

    es.close()
    return nc


_NC_CACHE = [None]


def kernel(L0_attn_x, L0_attn_Wq, L0_attn_bq, L0_attn_Wk, L0_attn_bk, L0_attn_Wv,
           L0_attn_bv, L0_ffn_ln_gamma, L0_ffn_ln_beta, L0_ffn_fc1_q, L0_ffn_fc1_scales,
           L0_ffn_fc1_bias, L0_ffn_fc2_q, L0_ffn_fc2_scales, L0_ffn_fc2_bias):
    x = np.ascontiguousarray(np.asarray(L0_attn_x, dtype=np.float32))

    def emaj(w):
        return np.ascontiguousarray(
            w.reshape(ND, 128, ND, 128).transpose(1, 2, 0, 3).reshape(128, ND * D))

    def h2maj(w):
        return np.ascontiguousarray(
            w.reshape(ND, 128, 2, 512).transpose(1, 2, 0, 3).reshape(128, 2 * 4096))

    wq = emaj(np.asarray(L0_attn_Wq, dtype=np.float32))
    wk = emaj(np.asarray(L0_attn_Wk, dtype=np.float32))
    wv = h2maj(np.asarray(L0_attn_Wv, dtype=np.float32))
    bq = np.asarray(L0_attn_bq, dtype=np.float32)
    bk = np.asarray(L0_attn_bk, dtype=np.float32)
    bv = np.asarray(L0_attn_bv, dtype=np.float32)
    gam = np.asarray(L0_ffn_ln_gamma, dtype=np.float32)
    bet = np.asarray(L0_ffn_ln_beta, dtype=np.float32)
    q1 = np.ascontiguousarray(np.asarray(L0_ffn_fc1_q, dtype=np.int32)
                              .reshape(ND, 128, 8, 512).transpose(0, 2, 1, 3))
    s1 = np.asarray(L0_ffn_fc1_scales, dtype=np.float32)
    b1 = np.asarray(L0_ffn_fc1_bias, dtype=np.float32)
    q2 = np.ascontiguousarray(np.asarray(L0_ffn_fc2_q, dtype=np.int32))
    s2 = np.asarray(L0_ffn_fc2_scales, dtype=np.float32)
    b2 = np.asarray(L0_ffn_fc2_bias, dtype=np.float32)

    bqT = np.ascontiguousarray(bq.reshape(ND, 128).T)
    bkT = np.ascontiguousarray(bk.reshape(ND, 128).T)
    b1T = np.ascontiguousarray(b1.reshape(NF, 128).T)
    gamb = np.ascontiguousarray(np.broadcast_to(gam[None, :], (128, D)))
    betb = np.ascontiguousarray(np.broadcast_to(bet[None, :], (128, D)))
    bvb = np.ascontiguousarray(np.broadcast_to(bv[None, :], (128, D)))
    b2b = np.ascontiguousarray(np.broadcast_to(b2[None, :], (128, D)))
    s1b = np.ascontiguousarray(np.broadcast_to(s1[:, None, :], (ND, 128, FF))
                               .reshape(ND, 128, 8, 512).transpose(0, 2, 1, 3))
    s2b = np.ascontiguousarray(np.broadcast_to(s2[:, None, :], (NF, 128, D)))
    mask = np.where(np.tril(np.ones((128, 128), dtype=bool)), np.float32(0.0),
                    np.float32(NEG_INF)).astype(np.float32)
    eyeb = np.eye(128).astype(ml_dtypes.bfloat16)

    if _NC_CACHE[0] is None:
        _NC_CACHE[0] = build_nc()
    nc = _NC_CACHE[0]

    common = dict(wq=wq, wk=wk, wv=wv, bqT=bqT, bkT=bkT, b1T=b1T, gam=gamb, bet=betb,
                  bvb=bvb, b2b=b2b, q1=q1, s1=s1b, q2=q2, s2=s2b, mask=mask, eyeb=eyeb)
    in_maps = [dict(common, x=np.ascontiguousarray(x[c])) for c in range(8)]
    res = run_bass_kernel_spmd(nc, in_maps, core_ids=list(range(8)))
    out = np.stack([np.asarray(res.results[c]["out"]) for c in range(8)], axis=0)
    return out.astype(np.float32)


# version 30 (final)
# speedup vs baseline: 1.0092x; 1.0092x over previous
import contextlib
import numpy as np
import ml_dtypes
import concourse.bass as bass
import concourse.mybir as mybir
from concourse.bass_utils import run_bass_kernel_spmd

F32 = mybir.dt.float32
BF16 = mybir.dt.bfloat16
I32 = mybir.dt.int32
AF = mybir.ActivationFunctionType
ALU = mybir.AluOpType
AX = mybir.AxisListType

T = 2048
D = 1024
FF = 4096
NT = 16
ND = 8
NF = 32
EPS = 1e-5
NEG_INF = -1e30
PAIRS = ((0, 0), (0, 1), (1, 0))


class Sem:
    def __init__(self, nc, es, name):
        self.h = es.enter_context(nc.semaphore(name))
        self.n = 0

    def inc(self, instr, k=1):
        instr.then_inc(self.h, k)
        self.n += k
        return self.n


def build_nc():
    nc = bass.Bass("TRN2", target_bir_lowering=False)
    es = contextlib.ExitStack()

    x_h = nc.dram_tensor("x", [T, D], F32, kind="ExternalInput").ap()
    wq_h = nc.dram_tensor("wq", [128, ND * D], F32, kind="ExternalInput").ap()
    wk_h = nc.dram_tensor("wk", [128, ND * D], F32, kind="ExternalInput").ap()
    wv_h = nc.dram_tensor("wv", [128, 2 * 4096], F32, kind="ExternalInput").ap()
    bqT_h = nc.dram_tensor("bqT", [128, ND], F32, kind="ExternalInput").ap()
    bkT_h = nc.dram_tensor("bkT", [128, ND], F32, kind="ExternalInput").ap()
    b1T_h = nc.dram_tensor("b1T", [128, NF], F32, kind="ExternalInput").ap()
    gam_h = nc.dram_tensor("gam", [128, D], F32, kind="ExternalInput").ap()
    bet_h = nc.dram_tensor("bet", [128, D], F32, kind="ExternalInput").ap()
    bvb_h = nc.dram_tensor("bvb", [128, D], F32, kind="ExternalInput").ap()
    b2b_h = nc.dram_tensor("b2b", [128, D], F32, kind="ExternalInput").ap()
    q1_h = nc.dram_tensor("q1", [ND, 8, 128, 512], I32, kind="ExternalInput").ap()
    s1_h = nc.dram_tensor("s1", [ND, 8, 128, 512], F32, kind="ExternalInput").ap()
    q2_h = nc.dram_tensor("q2", [FF, D], I32, kind="ExternalInput").ap()
    s2_h = nc.dram_tensor("s2", [NF, 128, D], F32, kind="ExternalInput").ap()
    mask_h = nc.dram_tensor("mask", [128, 128], F32, kind="ExternalInput").ap()
    eyeb_h = nc.dram_tensor("eyeb", [128, 128], BF16, kind="ExternalInput").ap()
    out_h = nc.dram_tensor("out", [T, D], F32, kind="ExternalOutput").ap()

    hxn = [nc.dram_tensor(f"hxn{i}", [T, D], BF16, kind="Internal").ap() for i in range(2)]
    hh = [nc.dram_tensor(f"hh{i}", [T, D], BF16, kind="Internal").ap() for i in range(2)]
    hqT = nc.dram_tensor("hqT", [NT, 2, 128, D], BF16, kind="Internal").ap()
    hattn = nc.dram_tensor("hattn", [NT, 128, D], F32, kind="Internal").ap()
    hy = nc.dram_tensor("hy", [4, 2, 2, 128, NF, 256], BF16, kind="Internal").ap()

    _rsv = nc.alloc_sbuf_tensor("arena_reserve", [128, 208704], mybir.dt.uint8)
    BASE = None
    for _a in nc.m.functions[0].allocations:
        if isinstance(_a, mybir.MemoryLocationSet) and _a.memorylocations[0].name == "arena_reserve":
            BASE = _a.memorylocations[0].addr
    assert BASE is not None and BASE + 208704 <= 229376, BASE

    def sb(name, offset, shape, dt):
        return nc.alloc_sbuf_tensor_at(name, shape, dt, offset=BASE + offset).ap()

    A_x = sb("A_x", 0, [128, NT * D], F32)
    A_kth = sb("A_kth", 0, [128, ND * T], BF16)
    A_ktl = sb("A_ktl", 32768, [128, ND * T], BF16)
    A_w1h = sb("A_w1h", 0, [128, ND * FF], BF16)
    A_w2h = sb("A_w2h", 0, [128, NF * D], BF16)

    C_wst = sb("C_wst", 65536, [128, ND * D], F32)
    C_wh = sb("C_wh", 98304, [128, ND * D], BF16)
    C_wl = sb("C_wl", 114688, [128, ND * D], BF16)
    C_vh = sb("C_vh", 65536, [128, NT * D], BF16)
    C_vl = sb("C_vl", 98304, [128, NT * D], BF16)
    C_w1l = sb("C_w1l", 65536, [128, ND * FF], BF16)
    C_w2l = sb("C_w2l", 65536, [128, NF * D], BF16)

    B_sh = [sb(f"B_sh{i}", 184320 + i * 2048, [128, D], BF16) for i in range(2)]
    B_sl = [sb(f"B_sl{i}", 188416 + i * 2048, [128, D], BF16) for i in range(2)]
    B_xph = [sb(f"B_xph{i}", 176128 + i * 2048, [128, D], BF16) for i in range(2)]
    B_xpl = [sb(f"B_xpl{i}", 180224 + i * 2048, [128, D], BF16) for i in range(2)]
    B_tf = sb("B_tf", 192512, [128, D], F32)
    B_xth = [sb(f"B_xth{i}", 131072 + i * 16384, [128, ND * 512], BF16) for i in range(2)]
    B_xtl = [sb(f"B_xtl{i}", 139264 + i * 16384, [128, ND * 512], BF16) for i in range(2)]
    B_qsh = sb("B_qsh", 167936, [128, ND * 512], BF16)
    B_qsl = sb("B_qsl", 176128, [128, ND * 512], BF16)
    B_wvh = sb("B_wvh", 163840, [128, ND * D], BF16)
    B_wvl = sb("B_wvl", 180224, [128, ND * D], BF16)

    B_sc = sb("B_sc", 131072, [128, T], F32)
    B_pf = [sb(f"B_pf{i}", 139264 + i * 2048, [128, 512], F32) for i in range(2)]
    B_ph = sb("B_ph", 143360, [128, T], BF16)
    B_pl = sb("B_pl", 147456, [128, T], BF16)
    B_pth = sb("B_pth", 151552, [128, T], BF16)
    B_ptl = sb("B_ptl", 155648, [128, T], BF16)
    B_qth = [sb(f"B_qth{i}", 159744 + i * 2048, [128, D], BF16) for i in range(2)]
    B_qtl = [sb(f"B_qtl{i}", 163840 + i * 2048, [128, D], BF16) for i in range(2)]
    B_ast = [sb(f"B_ast{i}", 167936 + i * 4096, [128, D], F32) for i in range(2)]

    B_hth = [sb(f"B_hth{i}", 131072 + i * 16384, [128, ND * 512], BF16) for i in range(2)]
    B_htl = [sb(f"B_htl{i}", 139264 + i * 16384, [128, ND * 512], BF16) for i in range(2)]
    B_yf = [sb(f"B_yf{i}", 163840 + i * 2048, [128, 512], F32) for i in range(2)]
    B_yst = [[sb(f"B_yst{s}{h}", 167936 + (s * 2 + h) * 4096, [128, 8 * 256], BF16)
              for h in range(2)] for s in range(2)]
    B_dqq = [sb(f"B_dqq{i}", [184320, 186368, 192512][i], [128, 512], I32) for i in range(3)]
    B_dqs = [sb(f"B_dqs{i}", [188416, 190464, 194560][i], [128, 512], F32) for i in range(3)]

    B_ygh = sb("B_ygh", 131072, [128, NF * 256], BF16)
    B_ygl = sb("B_ygl", 147456, [128, NF * 256], BF16)
    B_xt2 = [sb(f"B_xt2{i}", 163840 + i * 4096, [128, D], F32) for i in range(2)]
    B_at2 = [sb(f"B_at2{i}", 172032 + i * 4096, [128, D], F32) for i in range(2)]
    B_t0 = [sb(f"B_t0{i}", 180224 + i * 2048, [128, 512], F32) for i in range(2)]
    B_zb = [sb(f"B_zb{i}", 184320 + i * 2048, [128, 512], F32) for i in range(2)]
    B_ost = [sb(f"B_ost{i}", 188416 + i * 4096, [128, D], F32) for i in range(2)]
    B_dq2q = [sb(f"B_dq2q{i}", 163840 + i * 4096, [128, D], I32) for i in range(2)]
    B_dq2s = [sb(f"B_dq2s{i}", 172032 + i * 4096, [128, D], F32) for i in range(2)]
    B_dq2f = [sb(f"B_dq2f{i}", 180224 + i * 4096, [128, D], F32) for i in range(2)]

    M_mask = sb("M_mask", 196608, [128, 128], F32)
    M_eyeb = sb("M_eyeb", 197120, [128, 128], BF16)
    off = [197376]

    def st16(name, w=16):
        a = sb(name, off[0], [128, w], F32)
        off[0] += max(64, w * 4)
        return a

    M_sum1 = st16("M_sum1"); M_mu1 = st16("M_mu1"); M_ssa = st16("M_ssa")
    M_ssb = st16("M_ssb"); M_ss = st16("M_ss"); M_v1 = st16("M_v1")
    M_vv1 = st16("M_vv1"); M_rv1 = st16("M_rv1"); M_r1 = st16("M_r1")
    M_sum2 = st16("M_sum2"); M_mu2 = st16("M_mu2")
    M_vv2 = st16("M_vv2"); M_rv2 = st16("M_rv2"); M_r2 = st16("M_r2")
    M_negm = st16("M_negm"); M_m16 = st16("M_m16"); M_z16 = st16("M_z16")
    M_e16 = st16("M_e16"); M_c16 = st16("M_c16"); M_g16 = st16("M_g16")
    M_zc = st16("M_zc")
    M_msq = st16("M_msq"); M_msq2 = st16("M_msq2")
    M_col = st16("M_col", 1)
    M_negmg = st16("M_negmg", 1)
    M_Zb = st16("M_Zb", 1)
    M_rZb = st16("M_rZb", 1)
    M_bq32 = st16("M_bq32", ND)
    M_bkT = st16("M_bkT", ND)
    M_b1T = st16("M_b1T", NF)
    M_m11 = sb("M_m11", off[0], [1, 2], F32); off[0] += 64
    M_rows = sb("M_rows", off[0], [1, 128], F32); off[0] += 512
    M_pones = sb("M_pones", off[0], [1, 128], F32); off[0] += 512
    assert off[0] <= 200448, off[0]
    M_gam = sb("M_gam", 200448, [128, D], F32)
    M_bet = sb("M_bet", 204544, [128, D], F32)
    M_bvb = sb("M_bvb", 200448, [128, D], F32)
    M_b2b = sb("M_b2b", 204544, [128, D], F32)

    P = [nc.alloc_psum_tensor(f"P{i}", [128, 512], F32).ap() for i in range(6)]
    PTp = nc.alloc_psum_tensor("PTp", [128, 1024], BF16).ap()
    PS = nc.alloc_psum_tensor("PSp", [128, 512], F32).ap()

    ve = Sem(nc, es, "ve")
    ac = Sem(nc, es, "ac")
    gp = Sem(nc, es, "gp")
    mm = Sem(nc, es, "mm")
    l0 = Sem(nc, es, "l0")
    l0g = [Sem(nc, es, f"l0g{i}") for i in range(4)]
    lw = Sem(nc, es, "lw")
    lx = [Sem(nc, es, f"lx{i}") for i in range(2)]
    lq = [Sem(nc, es, f"lq{i}") for i in range(2)]
    lh = [Sem(nc, es, f"lh{i}") for i in range(2)]
    ldq = [Sem(nc, es, f"ldq{i}") for i in range(3)]
    ly = Sem(nc, es, "ly")
    lyq = [Sem(nc, es, f"lyq{i}") for i in range(4)]
    lxa = Sem(nc, es, "lxa")
    mmq = Sem(nc, es, "mmq")
    sxg = [Sem(nc, es, f"sxg{i}") for i in range(4)]
    aev = Sem(nc, es, "aev")
    lev = Sem(nc, es, "lev")
    lxp = Sem(nc, es, "lxp")
    shh = [Sem(nc, es, f"shh{i}") for i in range(2)]
    sq = Sem(nc, es, "sq")
    sa = [Sem(nc, es, f"sa{i}") for i in range(2)]
    sy4 = [Sem(nc, es, f"sy{i}") for i in range(4)]
    so = Sem(nc, es, "so")

    def vch(vector, instr):
        v = ve.inc(instr)
        vector.wait_ge(ve.h, v)

    def rd(eng, sem, out, in_, **kw):
        eng.dma_start(out=out, in_=in_, **kw).then_inc(sem.h, 16)
        sem.n += 16
        return sem.n

    def wr(eng, sem, out, in_):
        eng.dma_start(out=out, in_=in_).then_inc(sem.h, 16)
        sem.n += 16
        return sem.n

    with nc.Block() as block:
        @block.gpsimd
        def _(gpsimd):
            rd(gpsimd, l0, M_mask[:], mask_h)
            rd(gpsimd, l0, M_eyeb[:], eyeb_h)
            rd(gpsimd, l0, M_bq32[:], bqT_h)
            rd(gpsimd, l0, M_bkT[:], bkT_h)
            rd(gpsimd, l0, M_b1T[:], b1T_h)
            rd(gpsimd, l0, M_gam[:], gam_h)
            rd(gpsimd, l0, M_bet[:], bet_h)
            for j in range(4):
                rd(gpsimd, l0g[0], A_x[:, j * D:(j + 1) * D], x_h[j * 128:(j + 1) * 128, :])
            rd(gpsimd, lw, C_wst[:], wq_h)
            for j in range(4, NT):
                rd(gpsimd, l0g[j // 4], A_x[:, j * D:(j + 1) * D], x_h[j * 128:(j + 1) * 128, :])
        lw_wq = lw.n

        @block.vector
        def _(vector):
            vector.memset(M_rows[:], -1.0)
            vector.memset(M_pones[:], 1.0)

        ve_xnt = {}
        ac_r1g = {}
        mm_tb = {}
        xtb_evac = {}

        for g4 in range(4):
            gs = slice(g4 * 4, g4 * 4 + 4)

            @block.vector
            def _(vector, g4=g4):
                vector.wait_ge(l0g[g4].h, 64)
                if g4 == 0:
                    vector.wait_ge(l0.h, 7 * 16)
                for j in range(g4 * 4, g4 * 4 + 4):
                    i = vector.reduce_sum(M_sum1[:, j:j + 1], A_x[:, j * D:(j + 1) * D], axis=AX.X)
                ve.inc(i)
            ve_s1g = ve.n

            @block.scalar
            def _(scalar, g4=g4, gs=gs, ve_s1g=ve_s1g):
                scalar.wait_ge(l0g[g4].h, 64)
                for j in range(g4 * 4, g4 * 4 + 4):
                    if not (g4 == 0 and j == 0):
                        scalar.wait_ge(ac.h, ac.n)
                    scalar.activation(P[4][:, 0:512], A_x[:, j * D:j * D + 512], AF.Square,
                                      accum_out=M_ssa[:, j:j + 1])
                    i = scalar.activation(P[5][:, 0:512], A_x[:, j * D + 512:(j + 1) * D], AF.Square,
                                          accum_out=M_ssb[:, j:j + 1])
                    ac.inc(i)
                scalar.wait_ge(ve.h, ve_s1g)
                ac.inc(scalar.activation(M_mu1[:, gs], M_sum1[:, gs], AF.Copy, scale=1.0 / 1024.0))
            ac_mug = ac.n

            @block.vector
            def _(vector, gs=gs, ac_mug=ac_mug):
                vector.wait_ge(ac.h, ac_mug)
                vch(vector, vector.tensor_tensor(M_msq[:, gs], M_mu1[:, gs], M_mu1[:, gs],
                                                 ALU.mult))
                vch(vector, vector.tensor_add(M_ss[:, gs], M_ssa[:, gs], M_ssb[:, gs]))
                vch(vector, vector.scalar_tensor_tensor(M_v1[:, gs], M_ss[:, gs], 1.0 / 1024.0,
                                                        M_msq[:, gs], ALU.mult, ALU.subtract))
                ve.inc(vector.tensor_scalar_add(M_v1[:, gs], M_v1[:, gs], EPS))
            ve_v1g = ve.n

            @block.scalar
            def _(scalar, gs=gs, ve_v1g=ve_v1g):
                scalar.wait_ge(ve.h, ve_v1g)
                ac.inc(scalar.activation(M_vv1[:, gs], M_v1[:, gs], AF.Sqrt))
            ac_r1g[g4] = ac.n

            @block.vector
            def _(vector, g4=g4, gs=gs):
                vector.wait_ge(ac.h, ac_r1g[g4])
                vch(vector, vector.reciprocal(M_r1[:, gs], M_vv1[:, gs]))

            for j in range(g4 * 4, g4 * 4 + 4):
                @block.vector
                def _(vector, j=j):
                    s = j % 2
                    if j >= 2:
                        t = j - 1
                        vector.wait_ge(sxg[t // 4].h, 32 * ((t % 4) + 1))
                        if (j - 2) // 4 != t // 4:
                            vector.wait_ge(sxg[(j - 2) // 4].h, 128)
                        if (j - 2, 1) in mm_tb:
                            vector.wait_ge(mm.h, mm_tb[(j - 2, 1)])
                    aj = A_x[:, j * D:(j + 1) * D]
                    vch(vector, vector.tensor_scalar(aj, aj, M_mu1[:, j:j + 1], M_r1[:, j:j + 1],
                                                     ALU.subtract, ALU.mult))
                    vch(vector, vector.tensor_copy(B_sh[s][:], aj))
                    i = vector.tensor_tensor(B_sl[s][:], aj, B_sh[s][:], ALU.subtract)
                    ve.inc(i)
                    ve_xnt[j] = ve.n

                if j < 4:
                    for half in range(2):
                        @block.tensor
                        def _(tensor, j=j, half=half):
                            tensor.wait_ge(ve.h, ve_xnt[j])
                            prev = (j, half - 1) if half == 1 else (j - 1, 1)
                            if prev in xtb_evac:
                                tensor.wait_ge(ve.h, xtb_evac[prev])
                            srcb = (B_sh, B_sl)[half][j % 2]
                            for g in range(ND):
                                i = tensor.transpose(PTp[:, g * 128:(g + 1) * 128],
                                                     srcb[:, g * 128:(g + 1) * 128], M_eyeb[:])
                            mm.inc(i)
                            mm_tb[(j, half)] = mm.n

                        @block.vector
                        def _(vector, j=j, half=half):
                            vector.wait_ge(mm.h, mm_tb[(j, half)])
                            dstb = (B_xth, B_xtl)[half][0]
                            for g in range(ND):
                                i = vector.tensor_copy(
                                    dstb[:, g * 512 + j * 128:g * 512 + (j + 1) * 128],
                                    PTp[:, g * 128:(g + 1) * 128])
                            ve.inc(i)
                            xtb_evac[(j, half)] = ve.n

                @block.scalar
                def _(scalar, j=j, g4=g4):
                    s = j % 2
                    scalar.wait_ge(ve.h, ve_xnt[j])
                    wr(scalar, sxg[g4], hxn[0][j * 128:(j + 1) * 128, :], B_sh[s][:])
                    wr(scalar, sxg[g4], hxn[1][j * 128:(j + 1) * 128, :], B_sl[s][:])

            if g4 == 0:
                @block.scalar
                def _(scalar):
                    scalar.wait_ge(lw.h, lw_wq)
                    for e in range(ND):
                        i = scalar.activation(C_wh[:, e * D:(e + 1) * D],
                                              C_wst[:, e * D:(e + 1) * D], AF.Copy)
                        aev.inc(i)

                @block.vector
                def _(vector):
                    ab = aev.n - ND
                    for e in range(ND):
                        vector.wait_ge(aev.h, ab + e + 1)
                        i = vector.tensor_tensor(C_wl[:, e * D:(e + 1) * D],
                                                 C_wst[:, e * D:(e + 1) * D],
                                                 C_wh[:, e * D:(e + 1) * D], ALU.subtract)
                        lev.inc(i)

        def xnt_load(sync, tc, slot):
            sem = lx[slot]
            sync.wait_ge(sxg[tc].h, 128)
            for g in range(ND):
                rd(sync, sem, B_xth[slot][:, g * 512:(g + 1) * 512],
                   hxn[0][tc * 512:(tc + 1) * 512, g * 128:(g + 1) * 128], transpose=True)
                rd(sync, sem, B_xtl[slot][:, g * 512:(g + 1) * 512],
                   hxn[1][tc * 512:(tc + 1) * 512, g * 128:(g + 1) * 128], transpose=True)
            return sem.n

        def mm24(tensor, psum_ap, lhsT, rhsT, lslc, rslc):
            n = 0
            for g in range(ND):
                for (a, b) in PAIRS:
                    i = tensor.matmul(psum_ap, lhsT[a][lslc(g)], rhsT[b][rslc(g)],
                                      start=(n == 0), stop=(n == 23))
                    n += 1
            return i

        xnt_ld = {}
        mmtc = {}
        sq_tc = {}
        ve_red = {}
        red_plan = {1: range(0, 4), 2: range(4, 12), 3: range(12, 16)}


        WQ_LB = 0
        lo_val = {}
        wk_state = {}

        for tc in range(4):
            @block.sync
            def _(sync, tc=tc):
                if tc + 1 < 4:
                    if tc >= 1:
                        sync.wait_ge(mm.h, mmtc[tc - 1])
                    xnt_ld[("a", tc + 1)] = xnt_load(sync, tc + 1, (tc + 1) % 2)
                elif tc == 3:
                    sync.wait_ge(mm.h, mmtc[2])
                    xnt_ld[("b", 0)] = xnt_load(sync, 0, 0)

            @block.gpsimd
            def _(gpsimd, tc=tc):
                if tc == 1:
                    gpsimd.wait_ge(aev.h, WQ_LB + ND)
                    gpsimd.wait_ge(lev.h, WQ_LB + ND)
                    rd(gpsimd, lw, C_wst[:], wk_h)
                    wk_state["lw"] = lw.n

            lb_tc = lev.n

            @block.tensor
            def _(tensor, tc=tc, lb_tc=lb_tc):
                if tc == 0:
                    tensor.wait_ge(ve.h, xtb_evac[(3, 1)])
                else:
                    tensor.wait_ge(lx[tc % 2].h, xnt_ld[("a", tc)])
                slot = tc % 2
                for e in range(ND):
                    k = tc * ND + e
                    if tc == 0:
                        tensor.wait_ge(lev.h, WQ_LB + e + 1)
                    if k >= 4:
                        if e >= 4:
                            tensor.wait_ge(lev.h, lb_tc + (e - 4) + 1)
                        else:
                            tensor.wait_ge(lev.h, lo_val[k - 4])
                    i = mm24(tensor, P[k % 4][:, 0:512], (C_wh, C_wl),
                             (B_xth[slot], B_xtl[slot]),
                             lambda g, e=e: (slice(None), slice(e * D + g * 128, e * D + (g + 1) * 128)),
                             lambda g: (slice(None), slice(g * 512, (g + 1) * 512)))
                    mm.inc(i)
                mmtc[tc] = mm.n

            @block.scalar
            def _(scalar, tc=tc):
                base = mmtc[tc] - ND
                if tc >= 1:
                    scalar.wait_ge(sq.h, sq_tc[tc - 1])
                for e in range(ND):
                    k = tc * ND + e
                    scalar.wait_ge(mm.h, base + e + 1)
                    i = scalar.activation(B_qsh[:, e * 512:(e + 1) * 512], P[k % 4][:, 0:512],
                                          AF.Identity, bias=M_bq32[:, e:e + 1])
                    aev.inc(i)
            ab_a = aev.n - ND

            if tc == 3:
                @block.scalar
                def _(scalar, tc=tc):
                    scalar.wait_ge(lw.h, wk_state["lw"])
                    base = mmtc[3] - ND
                    for e in range(ND):
                        scalar.wait_ge(mm.h, base + e + 1)
                        i = scalar.activation(C_wh[:, e * D:(e + 1) * D],
                                              C_wst[:, e * D:(e + 1) * D], AF.Copy)
                        aev.inc(i)
                wk_state["ab"] = aev.n - ND

            @block.vector
            def _(vector, tc=tc, ab_a=ab_a):
                if tc >= 1:
                    vector.wait_ge(sq.h, sq_tc[tc - 1])
                for e in range(ND):
                    k = tc * ND + e
                    vector.wait_ge(aev.h, ab_a + e + 1)
                    i = vector.scalar_tensor_tensor(B_qsl[:, e * 512:(e + 1) * 512],
                                                    P[k % 4][:, 0:512], M_bq32[:, e:e + 1],
                                                    B_qsh[:, e * 512:(e + 1) * 512],
                                                    ALU.add, ALU.subtract)
                    lev.inc(i)
                    lo_val[k] = lev.n
            lev_a_tc = lev.n

            if tc == 3:
                @block.vector
                def _(vector):
                    for e in range(ND):
                        vector.wait_ge(aev.h, wk_state["ab"] + e + 1)
                        i = vector.tensor_tensor(C_wl[:, e * D:(e + 1) * D],
                                                 C_wst[:, e * D:(e + 1) * D],
                                                 C_wh[:, e * D:(e + 1) * D], ALU.subtract)
                        lev.inc(i)
                wk_state["lb"] = lev.n - ND

            if tc in red_plan:
                @block.vector
                def _(vector, tc=tc):
                    for t in red_plan[tc]:
                        i = vector.reduce_sum(M_sum2[:, t:t + 1], A_x[:, t * D:(t + 1) * D],
                                              axis=AX.X)
                        ve.inc(i)
                        ve_red[t] = ve.n

            @block.scalar
            def _(scalar, tc=tc, lev_a_tc=lev_a_tc):
                scalar.wait_ge(lev.h, lev_a_tc)
                qsh3 = B_qsh.rearrange("p (g w) -> p g w", g=ND)
                qsl3 = B_qsl.rearrange("p (g w) -> p g w", g=ND)
                for jj in range(4):
                    j = tc * 4 + jj
                    wr(scalar, sq, hqT[j, 0].rearrange("p (g u) -> p g u", g=ND),
                       qsh3[:, :, jj * 128:(jj + 1) * 128])
                    wr(scalar, sq, hqT[j, 1].rearrange("p (g u) -> p g u", g=ND),
                       qsl3[:, :, jj * 128:(jj + 1) * 128])
                sq_tc[tc] = sq.n

        mm_p1a = mmtc[3]
        P1A_HI_END = ab_a + ND
        P1A_LO_END = lo_val[31]
        WK_LB = wk_state["lb"]

        @block.scalar
        def _(scalar):
            scalar.wait_ge(ve.h, ve_red[15])
            ac.inc(scalar.activation(M_mu2[:], M_sum2[:], AF.Copy, scale=1.0 / 1024.0))
        ac_mu2 = ac.n

        @block.vector
        def _(vector):
            vector.wait_ge(ac.h, ac_mu2)
            vch(vector, vector.tensor_tensor(M_msq2[:], M_mu2[:], M_mu2[:], ALU.mult))
            vch(vector, vector.tensor_scalar(M_v1[:], M_v1[:], EPS, None, ALU.subtract))
            vch(vector, vector.tensor_tensor(M_v1[:], M_v1[:], M_r1[:], ALU.mult))
            vch(vector, vector.tensor_tensor(M_v1[:], M_v1[:], M_r1[:], ALU.mult))
            vch(vector, vector.tensor_tensor(M_v1[:], M_v1[:], M_msq2[:], ALU.subtract))
            ve.inc(vector.tensor_scalar_add(M_v1[:], M_v1[:], EPS))
        ve_v2 = ve.n

        @block.scalar
        def _(scalar):
            scalar.wait_ge(ve.h, ve_v2)
            ac.inc(scalar.activation(M_vv2[:], M_v1[:], AF.Sqrt))
        ac_sq2c = ac.n

        @block.vector
        def _(vector):
            vector.wait_ge(ac.h, ac_sq2c)
            ve.inc(vector.reciprocal(M_r2[:], M_vv2[:]))
        ve_r2 = ve.n
        ac_r2 = None

        mmtc_b = {}
        wv_state = {}

        for tc in range(4):
            @block.sync
            def _(sync, tc=tc):
                if tc + 1 < 4:
                    sync.wait_ge(mm.h, mmtc_b[tc - 1] if tc >= 1 else mm_p1a)
                    xnt_ld[("b", tc + 1)] = xnt_load(sync, tc + 1, (tc + 1) % 2)
                elif tc == 3:
                    sync.wait_ge(mm.h, mmtc_b[2])
                    xnt_ld[("c", 0)] = xnt_load(sync, 0, 0)

            @block.gpsimd
            def _(gpsimd, tc=tc):
                if tc == 0:
                    gpsimd.wait_ge(aev.h, wk_state["ab"] + ND)
                    gpsimd.wait_ge(lev.h, WK_LB + ND + 1)
                    rd(gpsimd, lw, C_wst[:, 0:2 * 4096], wv_h)
                    wv_state["lw"] = lw.n

            lb_tc = lev.n

            @block.tensor
            def _(tensor, tc=tc, lb_tc=lb_tc):
                tensor.wait_ge(lx[tc % 2].h, xnt_ld[("b", tc)])
                slot = tc % 2
                for e in range(ND):
                    k = tc * ND + e
                    if tc == 0:
                        tensor.wait_ge(lev.h, WK_LB + e + 1)
                    if k < 4:
                        tensor.wait_ge(lev.h, P1A_LO_END)
                        tensor.wait_ge(aev.h, P1A_HI_END)
                    else:
                        if e >= 4:
                            tensor.wait_ge(lev.h, lb_tc + (e - 4) + 1)
                        else:
                            tensor.wait_ge(lev.h, lo_val[32 + k - 4])
                    i = mm24(tensor, P[k % 4][:, 0:512], (C_wh, C_wl),
                             (B_xth[slot], B_xtl[slot]),
                             lambda g, e=e: (slice(None), slice(e * D + g * 128, e * D + (g + 1) * 128)),
                             lambda g: (slice(None), slice(g * 512, (g + 1) * 512)))
                    mm.inc(i)
                mmtc_b[tc] = mm.n

            @block.scalar
            def _(scalar, tc=tc):
                base = mmtc_b[tc] - ND
                if tc == 0:
                    scalar.wait_ge(ve.h, ve_red[7])
                for e in range(ND):
                    k = tc * ND + e
                    scalar.wait_ge(mm.h, base + e + 1)
                    dst = slice(e * T + tc * 512, e * T + (tc + 1) * 512)
                    i = scalar.activation(A_kth[:, dst], P[k % 4][:, 0:512],
                                          AF.Identity, bias=M_bkT[:, e:e + 1])
                    aev.inc(i)
            ab_b = aev.n - ND

            if tc == 1:
                @block.scalar
                def _(scalar):
                    scalar.wait_ge(lw.h, wv_state["lw"])
                    scalar.wait_ge(sq.h, sq_tc[3])
                    for h2 in range(2):
                        i = scalar.activation(B_wvh[:, h2 * 4096:(h2 + 1) * 4096],
                                              C_wst[:, h2 * 4096:(h2 + 1) * 4096], AF.Copy)
                        aev.inc(i)
                wv_state["ab"] = aev.n - 2

            @block.vector
            def _(vector, tc=tc, ab_b=ab_b):
                for e in range(ND):
                    k = tc * ND + e
                    vector.wait_ge(aev.h, ab_b + e + 1)
                    dst = slice(e * T + tc * 512, e * T + (tc + 1) * 512)
                    i = vector.scalar_tensor_tensor(A_ktl[:, dst], P[k % 4][:, 0:512],
                                                    M_bkT[:, e:e + 1], A_kth[:, dst],
                                                    ALU.add, ALU.subtract)
                    lev.inc(i)
                    lo_val[32 + k] = lev.n
            if tc == 1:
                @block.vector
                def _(vector):
                    vector.wait_ge(sxg[3].h, 128)
                    vector.wait_ge(sq.h, sq_tc[3])
                    for h2 in range(2):
                        vector.wait_ge(aev.h, wv_state["ab"] + h2 + 1)
                        i = vector.tensor_tensor(B_wvl[:, h2 * 4096:(h2 + 1) * 4096],
                                                 C_wst[:, h2 * 4096:(h2 + 1) * 4096],
                                                 B_wvh[:, h2 * 4096:(h2 + 1) * 4096], ALU.subtract)
                        lev.inc(i)
                wv_state["lb"] = lev.n - 2

        mm_p1b = mmtc_b[3]
        P1B_HI_END = ab_b + ND
        P1B_LO_END = lev.n
        WV_LB = wv_state["lb"]

        mmv = {}

        for tc in range(4):
            @block.sync
            def _(sync, tc=tc):
                if tc + 1 < 4:
                    sync.wait_ge(mm.h, mmv[tc - 1] if tc >= 1 else mm_p1b)
                    xnt_ld[("c", tc + 1)] = xnt_load(sync, tc + 1, (tc + 1) % 2)

            lb_tc = lev.n

            @block.tensor
            def _(tensor, tc=tc, lb_tc=lb_tc):
                tensor.wait_ge(lx[tc % 2].h, xnt_ld[("c", tc)])
                slot = tc % 2
                for jj in range(4):
                    for h2 in range(2):
                        k = (tc * 4 + jj) * 2 + h2
                        m = jj * 2 + h2
                        if k == 0:
                            tensor.wait_ge(lev.h, WV_LB + 1)
                        elif k == 1:
                            tensor.wait_ge(lev.h, WV_LB + 2)
                        if k < 4:
                            tensor.wait_ge(lev.h, P1B_LO_END)
                            tensor.wait_ge(aev.h, P1B_HI_END)
                        else:
                            if m >= 4:
                                tensor.wait_ge(lev.h, lb_tc + (m - 4) + 1)
                            else:
                                tensor.wait_ge(lev.h, lo_val[64 + k - 4])
                        i = mm24(tensor, P[k % 4][:, 0:512], (B_xth[slot], B_xtl[slot]),
                                 (B_wvh, B_wvl),
                                 lambda g, jj=jj: (slice(None), slice(g * 512 + jj * 128, g * 512 + (jj + 1) * 128)),
                                 lambda g, h2=h2: (slice(None), slice(h2 * 4096 + g * 512, h2 * 4096 + (g + 1) * 512)))
                        mm.inc(i)
                mmv[tc] = mm.n

            @block.scalar
            def _(scalar, tc=tc):
                base = mmv[tc] - 8
                for jj in range(4):
                    for h2 in range(2):
                        k = (tc * 4 + jj) * 2 + h2
                        m = jj * 2 + h2
                        scalar.wait_ge(mm.h, base + m + 1)
                        if k == 0:
                            scalar.wait_ge(lev.h, WV_LB + 2)
                        dst = slice((tc * 4 + jj) * D + h2 * 512, (tc * 4 + jj) * D + (h2 + 1) * 512)
                        i = scalar.activation(C_vh[:, dst], P[k % 4][:, 0:512], AF.Copy)
                        aev.inc(i)
            ab_c = aev.n - 8

            @block.vector
            def _(vector, tc=tc, ab_c=ab_c):
                for jj in range(4):
                    for h2 in range(2):
                        k = (tc * 4 + jj) * 2 + h2
                        m = jj * 2 + h2
                        vector.wait_ge(aev.h, ab_c + m + 1)
                        dst = slice((tc * 4 + jj) * D + h2 * 512, (tc * 4 + jj) * D + (h2 + 1) * 512)
                        i = vector.tensor_tensor(C_vl[:, dst], P[k % 4][:, 0:512], C_vh[:, dst],
                                                 ALU.subtract)
                        lev.inc(i)
                        lo_val[64 + k] = lev.n

        mm_p1c = mmv[3]
        AEV_P1 = aev.n
        LEV_P1 = lev.n

        qt_ld = {}
        mm_sc = {}
        ac_scev = {}
        ve_msk = {}
        ac_exp = {}
        ve_pspl = {}
        ve_ptev = {}
        mm_pv = {}
        ve_avev = {}
        sa_t = {}

        lxp2 = [Sem(nc, es, f"lxp2{i}") for i in range(2)]
        xp_ld = {}
        xp_consumed = {0: None, 1: None}
        ve_xp = {}
        gp_a = {}
        ve_b = {}
        gp_c = {}
        ac_sq2u = {}
        ve_h2 = {}
        p0b_state = {"ve_r2": None}

        def emit_pass2_unit(u):
            t = u
            slot = u % 2

            @block.sync
            def _(sync, u=u, t=t, slot=slot):
                if xp_consumed[slot] is not None:
                    s_, v_ = xp_consumed[slot]
                    sync.wait_ge(s_.h, v_)
                rd(sync, lxp2[slot], B_xph[slot][:], hxn[0][t * 128:(t + 1) * 128, :])
                xp_ld[16 + u] = rd(sync, lxp2[slot], B_xpl[slot][:], hxn[1][t * 128:(t + 1) * 128, :])

            @block.vector
            def _(vector, u=u, t=t, slot=slot):
                vector.wait_ge(lxp2[slot].h, xp_ld[16 + u])
                if u == 0:
                    vector.wait_ge(ve.h, p0b_state["ve_r2"])
                if u >= 1:
                    vector.wait_ge(ve.h, ve_h2[u - 1])
                if u >= 2:
                    vector.wait_ge(shh[u % 2].h, 32 * ((u - 2) // 2 + 1))
                vch(vector, vector.tensor_add(B_tf[:], B_xph[slot][:], B_xpl[slot][:]))
                vch(vector, vector.tensor_scalar(B_tf[:], B_tf[:], M_mu2[:, t:t + 1],
                                                 M_r2[:, t:t + 1], ALU.subtract, ALU.mult))
                vch(vector, vector.tensor_tensor(B_tf[:], B_tf[:], M_gam[:], ALU.mult))
                vch(vector, vector.tensor_tensor(B_tf[:], B_tf[:], M_bet[:], ALU.add))
                vch(vector, vector.tensor_copy(B_sh[slot][:], B_tf[:]))
                i = vector.tensor_tensor(B_sl[slot][:], B_tf[:], B_sh[slot][:], ALU.subtract)
                ve.inc(i)
                ve_h2[u] = ve.n
                xp_consumed[slot] = (ve, ve.n)

            @block.scalar
            def _(scalar, u=u, t=t, slot=slot):
                scalar.wait_ge(ve.h, ve_h2[u])
                wr(scalar, shh[slot], hh[0][t * 128:(t + 1) * 128, :], B_sh[slot][:])
                wr(scalar, shh[slot], hh[1][t * 128:(t + 1) * 128, :], B_sl[slot][:])

        p0b_state["ve_r2"] = ve_r2

        def emit_p0b(j):
            emit_pass2_unit(j)

        mm_tr = {}
        nch_of = lambda t: ((t + 1) * 128 + 511) // 512
        w_of = lambda t, c: min(512, (t + 1) * 128 - c * 512)

        @block.sync
        def _(sync):
            sync.wait_ge(sq.h, sq_tc[3])
            sync.wait_ge(mm.h, mm_p1c)
            rd(sync, lq[0], B_qth[0][:], hqT[0, 0])
            qt_ld[0] = rd(sync, lq[0], B_qtl[0][:], hqT[0, 1])

        def emit_exp(t, ch):
            @block.scalar
            def _(scalar, t=t, ch=ch):
                scalar.wait_ge(ve.h, ve_msk[t])
                if ch >= 2:
                    scalar.wait_ge(ve.h, ve_pspl[(t, ch - 2)])
                w = w_of(t, ch)
                i = scalar.activation(B_pf[ch % 2][:, 0:w], B_sc[:, ch * 512:ch * 512 + w],
                                      AF.Exp, bias=M_negm[:, t:t + 1],
                                      accum_out=M_zc[:, ch:ch + 1])
                ac.inc(i)
                ac_exp[(t, ch)] = ac.n

        def emit_split(t, ch):
            @block.vector
            def _(vector, t=t, ch=ch):
                vector.wait_ge(ac.h, ac_exp[(t, ch)])
                if ch == 0 and t >= 1:
                    vector.wait_ge(mm.h, mm_tr[t - 1])
                w = w_of(t, ch)
                cs = slice(ch * 512, ch * 512 + w)
                vch(vector, vector.tensor_copy(B_ph[:, cs], B_pf[ch % 2][:, 0:w]))
                i = vector.tensor_tensor(B_pl[:, cs], B_pf[ch % 2][:, 0:w], B_ph[:, cs],
                                         ALU.subtract)
                ve.inc(i)
                ve_pspl[(t, ch)] = ve.n

        def emit_evac(j, ch, after_exp=None):
            @block.scalar
            def _(scalar, j=j, ch=ch, after_exp=after_exp):
                scalar.wait_ge(mm.h, mm_sc[(j, ch)])
                if after_exp is not None:
                    scalar.wait_ge(ac.h, after_exp)
                w = w_of(j, ch)
                i = scalar.activation(B_sc[:, ch * 512:ch * 512 + w], P[ch % 2][:, 0:w], AF.Copy,
                                      scale=1.0 / 32.0)
                ac.inc(i)
                ac_scev[(j, ch)] = ac.n

        def emit_z16(t):
            @block.vector
            def _(vector, t=t):
                i = vector.reduce_sum(M_z16[:, t:t + 1], M_zc[:, 0:nch_of(t)], axis=AX.X)
                ve.inc(i)

        def emit_tr(t):
            nblk = t + 1
            for half in range(2):
                srcb = B_ph if half == 0 else B_pl
                dstb = B_pth if half == 0 else B_ptl
                for g0 in range(0, nblk, 8):
                    gn = min(8, nblk - g0)

                    @block.tensor
                    def _(tensor, t=t, half=half, g0=g0, gn=gn, srcb=srcb, nblk=nblk):
                        tensor.wait_ge(ve.h, ve_pspl[(t, ((g0 + gn) * 128 - 1) // 512)])
                        if g0 >= 8:
                            prevk = (t, half, g0 - 8)
                        elif half == 1:
                            prevk = (t, 0, ((nblk - 1) // 8) * 8)
                        elif t >= 1:
                            prevk = (t - 1, 1, ((t - 1) // 8) * 8)
                        else:
                            prevk = None
                        if prevk is not None and prevk in ve_ptev:
                            tensor.wait_ge(ve.h, ve_ptev[prevk])
                        for b in range(g0, g0 + gn):
                            i = tensor.transpose(PTp[:, (b - g0) * 128:(b - g0 + 1) * 128],
                                                 srcb[:, b * 128:(b + 1) * 128], M_eyeb[:])
                        mm.inc(i)

                    @block.vector
                    def _(vector, t=t, half=half, g0=g0, gn=gn, dstb=dstb):
                        vector.wait_ge(mm.h, mm.n)
                        i = vector.tensor_copy(dstb[:, g0 * 128:(g0 + gn) * 128], PTp[:, 0:gn * 128])
                        ve.inc(i)
                        ve_ptev[(t, half, g0)] = ve.n
            mm_tr[t] = mm.n

        def emit_pv(t):
            nblk = t + 1

            @block.tensor
            def _(tensor, t=t, nblk=nblk):
                tensor.wait_ge(ve.h, ve_ptev[(t, 1, ((nblk - 1) // 8) * 8)])
                if t >= 1:
                    tensor.wait_ge(ve.h, ve_avev[t - 1])
                for h2 in range(2):
                    n = 0
                    for b in range(nblk):
                        for (a, c) in PAIRS:
                            i = tensor.matmul(P[2 + h2][:, 0:512],
                                              (B_pth, B_ptl)[a][:, b * 128:(b + 1) * 128],
                                              (C_vh, C_vl)[c][:, b * D + h2 * 512:b * D + (h2 + 1) * 512],
                                              start=(n == 0), stop=(n == 3 * nblk - 1))
                            n += 1
                    mm.inc(i)
                    mm_pv[(t, h2)] = mm.n

        def emit_avev(t):
            @block.vector
            def _(vector, t=t):
                vector.wait_ge(mm.h, mm_pv[(t, 1)])
                if t >= 2:
                    vector.wait_ge(sa[t % 2].h, 16 * ((t - 2) // 2 + 1))
                s = t % 2
                vector.tensor_copy(B_ast[s][:, 0:512], P[2][:, 0:512])
                i = vector.tensor_copy(B_ast[s][:, 512:1024], P[3][:, 0:512])
                ve.inc(i)
                ve_avev[t] = ve.n

        def emit_spill(t):
            @block.scalar
            def _(scalar, t=t):
                scalar.wait_ge(ve.h, ve_avev[t])
                wr(scalar, sa[t % 2], hattn[t], B_ast[t % 2][:])
                sa_t[t] = (sa[0].n, sa[1].n)

        for j in range(NT):
            nch = nch_of(j)
            pnch = nch_of(j - 1) if j >= 1 else 0

            @block.sync
            def _(sync, j=j):
                if j + 1 < NT:
                    if j >= 1:
                        sync.wait_ge(mm.h, mm_sc[(j - 1, nch_of(j - 1) - 1)])
                    s1_ = (j + 1) % 2
                    rd(sync, lq[s1_], B_qth[s1_][:], hqT[j + 1, 0])
                    qt_ld[j + 1] = rd(sync, lq[s1_], B_qtl[s1_][:], hqT[j + 1, 1])

            acb = ac.n
            for ch in range(nch):
                @block.tensor
                def _(tensor, j=j, ch=ch, nch=nch, acb=acb, pnch=pnch):
                    if j == 0 and ch == 0:
                        tensor.wait_ge(lev.h, LEV_P1)
                        tensor.wait_ge(aev.h, AEV_P1)
                    if ch == 0:
                        tensor.wait_ge(lq[j % 2].h, qt_ld[j])
                        if j >= 1:
                            tensor.wait_ge(ac.h, ac_scev[(j - 1, nch_of(j - 1) - 1)])
                    if ch >= 2:
                        tensor.wait_ge(ac.h, acb + pnch + (ch - 2) + 1)
                    slot = j % 2
                    w = w_of(j, ch)
                    i = mm24(tensor, P[ch % 2][:, 0:w], (B_qth[slot], B_qtl[slot]),
                             (A_kth, A_ktl),
                             lambda g: (slice(None), slice(g * 128, (g + 1) * 128)),
                             lambda g, ch=ch, w=w: (slice(None), slice(g * T + ch * 512, g * T + ch * 512 + w)))
                    mm.inc(i)
                    mm_sc[(j, ch)] = mm.n
            for ch in range(pnch):
                emit_exp(j - 1, ch)
                emit_split(j - 1, ch)
            for ch in range(nch):
                emit_evac(j, ch, after_exp=ac_exp[(j - 1, ch)] if ch < pnch else None)
            if j >= 1:
                emit_z16(j - 1)

            @block.vector
            def _(vector, j=j, nch=nch):
                vector.wait_ge(ac.h, ac_scev[(j, nch - 1)])
                dg = slice(j * 128, j * 128 + 128)
                vch(vector, vector.tensor_tensor(B_sc[:, dg], B_sc[:, dg], M_mask[:], ALU.add))
                i = vector.reduce_max(M_negm[:, j:j + 1], B_sc[:, 0:(j + 1) * 128],
                                      axis=AX.X, negate=True)
                ve.inc(i)
                ve_msk[j] = ve.n

            if j >= 1:
                emit_tr(j - 1)
            if j >= 2:
                emit_p0b(j - 2)

            if j >= 1:
                emit_pv(j - 1)
                emit_avev(j - 1)
                emit_spill(j - 1)

        fl = NT - 1
        for ch in range(nch_of(fl)):
            emit_exp(fl, ch)
            emit_split(fl, ch)
        emit_z16(fl)
        emit_tr(fl)
        emit_p0b(fl - 1)
        emit_p0b(fl)
        emit_pv(fl)
        emit_avev(fl)
        emit_spill(fl)

        shh_total = (shh[0].n, shh[1].n)

        @block.gpsimd
        def _(gpsimd):
            gpsimd.wait_ge(ve.h, ve.n)
            gp.inc(gpsimd.tensor_scalar_mul(M_m16[:], M_negm[:], -1.0))
        gp_m16 = gp.n

        @block.vector
        def _(vector):
            vector.wait_ge(gp.h, gp_m16)
            ve.inc(vector.reduce_max(M_col[:], M_m16[:], axis=AX.X))
        ve_mcol = ve.n

        @block.gpsimd
        def _(gpsimd):
            gpsimd.wait_ge(ve.h, ve_mcol)
            gp.inc(gpsimd.tensor_reduce(M_m11[:1, 0:1], M_col[:], AX.C, ALU.max))
        gp_m11 = gp.n

        @block.tensor
        def _(tensor):
            tensor.wait_ge(gp.h, gp_m11)
            mm.inc(tensor.matmul(PS[:, 0:1], M_rows[:], M_m11[:1, 0:1], start=True, stop=True))
        mm_negmg = mm.n

        @block.vector
        def _(vector):
            vector.wait_ge(mm.h, mm_negmg)
            ve.inc(vector.tensor_copy(M_negmg[:], PS[:, 0:1]))
        ve_negmg = ve.n

        @block.scalar
        def _(scalar):
            scalar.wait_ge(ve.h, ve_negmg)
            ac.inc(scalar.activation(M_e16[:], M_m16[:], AF.Exp, bias=M_negmg[:]))
        ac_e16 = ac.n

        @block.gpsimd
        def _(gpsimd):
            gpsimd.wait_ge(ac.h, ac_e16)
            gp.inc(gpsimd.tensor_mul(M_ss[:], M_e16[:], M_z16[:]))
        gp_zsc = gp.n

        @block.vector
        def _(vector):
            vector.wait_ge(gp.h, gp_zsc)
            ve.inc(vector.reduce_sum(M_col[:], M_ss[:], axis=AX.X))
        ve_zcol = ve.n

        @block.gpsimd
        def _(gpsimd):
            gpsimd.wait_ge(ve.h, ve_zcol)
            gp.inc(gpsimd.tensor_reduce(M_m11[:1, 1:2], M_col[:], AX.C, ALU.add))
        gp_z11 = gp.n

        @block.tensor
        def _(tensor):
            tensor.wait_ge(gp.h, gp_z11)
            mm.inc(tensor.matmul(PS[:, 1:2], M_pones[:], M_m11[:1, 1:2], start=True, stop=True))
        mm_zbb = mm.n

        @block.scalar
        def _(scalar):
            scalar.wait_ge(mm.h, mm_zbb)
            ac.inc(scalar.activation(M_Zb[:], PS[:, 1:2], AF.Copy))
        ac_zbb = ac.n

        @block.vector
        def _(vector):
            vector.wait_ge(ac.h, ac_zbb)
            ve.inc(vector.reciprocal(M_rZb[:], M_Zb[:]))
        ve_rzb = ve.n

        @block.scalar
        def _(scalar):
            scalar.wait_ge(ve.h, ve_rzb)
            ac.inc(scalar.activation(M_c16[:], M_e16[:], AF.Copy, scale=M_rZb[:]))
        ac_c16 = ac.n

        @block.gpsimd
        def _(gpsimd):
            gpsimd.wait_ge(ac.h, ac_c16)
            gp.inc(gpsimd.tensor_mul(M_g16[:], M_c16[:], M_z16[:]))
        gp_g16 = gp.n

        dq_ld = {}
        ve_dq = {}
        ve_dqblk = {}

        ht_ld = {}
        mm_y = {}
        ac_y = {}
        ve_y = {}
        sy_t = {}
        mm_tc_end = {}

        @block.sync
        def _(sync):
            sync.wait_ge(shh[0].h, shh_total[0])
            sync.wait_ge(shh[1].h, shh_total[1])
            sync.wait_ge(ve.h, ve_pspl[(NT - 1, 3)])
            sync.wait_ge(mm.h, mm_tr[NT - 1])
            sync.wait_ge(ac.h, ac_exp[(NT - 1, 3)])
            for g in range(ND):
                rd(sync, lh[0], B_hth[0][:, g * 512:(g + 1) * 512],
                   hh[0][0:512, g * 128:(g + 1) * 128], transpose=True)
                rd(sync, lh[0], B_htl[0][:, g * 512:(g + 1) * 512],
                   hh[1][0:512, g * 128:(g + 1) * 128], transpose=True)
        ht_ld[0] = lh[0].n


        ve_stt1 = {}
        ac_cp1 = {}
        PD1 = [P[4], P[5], PS]

        def emit_dq1_unit(fb, g):
            k = fb * ND + g

            @block.sync
            def _(sync, g=g, fb=fb, k=k):
                if k == 0:
                    sync.wait_ge(lev.h, LEV_P1)
                    sync.wait_ge(shh[0].h, shh_total[0])
                    sync.wait_ge(shh[1].h, shh_total[1])
                if k >= 3:
                    sync.wait_ge(ve.h, ve_stt1[k - 3])
                rd(sync, ldq[k % 3], B_dqq[k % 3][:], q1_h[g, fb])
                dq_ld[k] = rd(sync, ldq[k % 3], B_dqs[k % 3][:], s1_h[g, fb])

            @block.vector
            def _(vector, g=g, fb=fb, k=k):
                vector.wait_ge(ldq[k % 3].h, dq_ld[k])
                vch(vector, vector.scalar_tensor_tensor(PD1[k % 3][:, 0:512], B_dqq[k % 3][:],
                                                        -128.0, B_dqs[k % 3][:],
                                                        ALU.add, ALU.mult))
                ve_stt1[k] = ve.n
                dst = slice(g * FF + fb * 512, g * FF + (fb + 1) * 512)
                vch(vector, vector.tensor_copy(A_w1h[:, dst], PD1[k % 3][:, 0:512]))
                i = vector.tensor_tensor(C_w1l[:, dst], PD1[k % 3][:, 0:512], A_w1h[:, dst],
                                         ALU.subtract)
                ve.inc(i)
                ve_dq[k] = ve.n


        def emit_fc1_unit(tc, fc):
            k = tc * NF + fc

            @block.tensor
            def _(tensor, tc=tc, fc=fc, k=k):
                if fc == 0:
                    tensor.wait_ge(lh[tc % 2].h, ht_ld[tc])
                if tc == 0:
                    tensor.wait_ge(ve.h, ve_dqblk[fc // 4])
                if k >= 2:
                    tensor.wait_ge(ac.h, ac_y[k - 2])
                slot = tc % 2
                i = mm24(tensor, P[k % 2][:, 0:512], (A_w1h, C_w1l),
                         (B_hth[slot], B_htl[slot]),
                         lambda g, fc=fc: (slice(None), slice(g * FF + fc * 128, g * FF + (fc + 1) * 128)),
                         lambda g: (slice(None), slice(g * 512, (g + 1) * 512)))
                mm.inc(i)
                mm_y[k] = mm.n

            @block.scalar
            def _(scalar, fc=fc, k=k):
                scalar.wait_ge(mm.h, mm_y[k])
                if k >= 2:
                    scalar.wait_ge(gp.h, ve_y[k - 2])
                i = scalar.activation(B_yf[k % 2][:], P[k % 2][:, 0:512], AF.Silu,
                                      bias=M_b1T[:, fc:fc + 1])
                ac.inc(i)
                ac_y[k] = ac.n

            @block.gpsimd
            def _(gpsimd, tc=tc, fc=fc, k=k):
                gpsimd.wait_ge(ac.h, ac_y[k])
                if k == 0:
                    gpsimd.wait_ge(sa[0].h, sa_t[NT - 1][0])
                    gpsimd.wait_ge(sa[1].h, sa_t[NT - 1][1])
                if fc % 8 == 0 and k >= 8:
                    tp, vp = sy_t[k - 1]
                    gpsimd.wait_ge(sy4[tp].h, vp)
                c0 = (fc % 8) * 256
                gpsimd.tensor_copy(B_yst[0][0][:, c0:c0 + 256], B_yf[k % 2][:, 0:256])
                i = gpsimd.tensor_copy(B_yst[1][0][:, c0:c0 + 256], B_yf[k % 2][:, 256:512])
                v_ = gp.inc(i)
                gpsimd.wait_ge(gp.h, v_)
                for s2b in range(2):
                    i = gpsimd.tensor_tensor(B_yst[s2b][1][:, c0:c0 + 256],
                                             B_yf[k % 2][:, s2b * 256:(s2b + 1) * 256],
                                             B_yst[s2b][0][:, c0:c0 + 256], ALU.subtract)
                gp.inc(i)
                ve_y[k] = gp.n

            if fc % 8 == 7:
                @block.scalar
                def _(scalar, tc=tc, fc=fc, k=k):
                    scalar.wait_ge(gp.h, ve_y[k])
                    fg = fc // 8
                    for s2b in range(2):
                        for half in range(2):
                            wr(scalar, sy4[tc],
                               hy[tc, s2b, half, :, fg * 8:(fg + 1) * 8],
                               B_yst[s2b][half].rearrange("p (f u) -> p f u", f=8))
                    sy_t[k] = (tc, sy4[tc].n)

        def emit_ht_prefetch(tc):
            @block.sync
            def _(sync, tc=tc):
                if tc == 1:
                    sync.wait_ge(ve.h, ve_avev[NT - 1])
                if tc >= 2:
                    sync.wait_ge(mm.h, mm_tc_end[tc - 2])
                s1_ = tc % 2
                for g in range(ND):
                    rd(sync, lh[s1_], B_hth[s1_][:, g * 512:(g + 1) * 512],
                       hh[0][tc * 512:(tc + 1) * 512, g * 128:(g + 1) * 128], transpose=True)
                    rd(sync, lh[s1_], B_htl[s1_][:, g * 512:(g + 1) * 512],
                       hh[1][tc * 512:(tc + 1) * 512, g * 128:(g + 1) * 128], transpose=True)
                ht_ld[tc] = lh[s1_].n

        for fb in range(8):
            for g in range(ND):
                emit_dq1_unit(fb, g)
            ve_dqblk[fb] = ve.n
            if fb == 1:
                emit_ht_prefetch(1)
            if fb >= 1:
                for fc in range(4 * (fb - 1), 4 * (fb - 1) + 4):
                    emit_fc1_unit(0, fc)
        for fc in range(28, 32):
            emit_fc1_unit(0, fc)
        mm_tc_end[0] = mm.n

        for tc in range(1, 4):
            if tc + 1 < 4:
                emit_ht_prefetch(tc + 1)
            for fc in range(NF):
                emit_fc1_unit(tc, fc)
            mm_tc_end[tc] = mm.n

        ve_w1 = ve.n

        dq2_ld = {}
        lybb = [0]
        ve_stt2 = {}
        ac_cp2 = {}
        ve_dq2 = {}

        yg_ld = {}
        yq_ld = {}
        xa_ld = {}
        mm_z = {}
        mm_zh = {}
        ve_cmb = {}
        ve_cmbh = {}
        so_t = {}

        def ygrp_load(sync, grp, chase=False):
            tc_, sub_ = grp // 2, grp % 2
            sync.wait_ge(sy4[tc_].h, 256)
            for q4 in range(4):
                if chase:
                    if q4 < 3:
                        sync.wait_ge(mmq.h, (grp - 1) * 3 + q4 + 1)
                    else:
                        sync.wait_ge(mm.h, mm_z[grp - 1])
                for half in range(2):
                    dstb = (B_ygh, B_ygl)[half]
                    rd(sync, lyq[q4],
                       dstb.rearrange("p (f u) -> p f u", f=NF)[:, q4 * 8:(q4 + 1) * 8, :],
                       hy[tc_, sub_, half, :, q4 * 8:(q4 + 1) * 8])
                yq_ld[(grp, q4)] = lyq[q4].n

        @block.sync
        def _(sync):
            sync.wait_ge(mm.h, mm_tc_end[2])
            sync.wait_ge(sy4[0].h, 256)
            for q4 in range(4):
                rd(sync, lyq[q4],
                   B_ygh.rearrange("p (f u) -> p f u", f=NF)[:, q4 * 8:(q4 + 1) * 8, :],
                   hy[0, 0, 0, :, q4 * 8:(q4 + 1) * 8])

        @block.sync
        def _(sync):
            sync.wait_ge(mm.h, mm_tc_end[3])
            sync.wait_ge(sa[0].h, sa_t[NT - 1][0])
            sync.wait_ge(sa[1].h, sa_t[NT - 1][1])
            for q4 in range(4):
                rd(sync, lyq[q4],
                   B_ygl.rearrange("p (f u) -> p f u", f=NF)[:, q4 * 8:(q4 + 1) * 8, :],
                   hy[0, 0, 1, :, q4 * 8:(q4 + 1) * 8])
                yq_ld[(0, q4)] = lyq[q4].n


        for fc in range(NF):
            @block.sync
            def _(sync, fc=fc):
                if fc == 0:
                    sync.wait_ge(mm.h, mm_tc_end[3])
                    sync.wait_ge(ve.h, ve.n)
                    sync.wait_ge(gp.h, gp.n)
                    sync.wait_ge(ac.h, ac_y[4 * NF - 1])
                    for tci in range(4):
                        sync.wait_ge(sy4[tci].h, 256)
                    rd(sync, ly, M_bvb[:], bvb_h)
                    lybb[0] = rd(sync, ly, M_b2b[:], b2b_h)
                if fc >= 2:
                    sync.wait_ge(ve.h, ve_dq2[fc - 2])
                rd(sync, ldq[fc % 2], B_dq2q[fc % 2][:], q2_h[fc * 128:(fc + 1) * 128, :])
                dq2_ld[fc] = rd(sync, ldq[fc % 2], B_dq2s[fc % 2][:], s2_h[fc])

            @block.vector
            def _(vector, fc=fc):
                vector.wait_ge(ldq[fc % 2].h, dq2_ld[fc])
                if fc >= 2:
                    vector.wait_ge(ac.h, ac_cp2[fc - 2])
                i = vector.scalar_tensor_tensor(B_dq2f[fc % 2][:], B_dq2q[fc % 2][:], -128.0,
                                                B_dq2s[fc % 2][:], ALU.add, ALU.mult)
                ve.inc(i)
                ve_stt2[fc] = ve.n

            @block.scalar
            def _(scalar, fc=fc):
                scalar.wait_ge(ve.h, ve_stt2[fc])
                dst = slice(fc * D, (fc + 1) * D)
                i = scalar.activation(A_w2h[:, dst], B_dq2f[fc % 2][:], AF.Copy)
                ac.inc(i)
                ac_cp2[fc] = ac.n

            @block.vector
            def _(vector, fc=fc):
                vector.wait_ge(ac.h, ac_cp2[fc])
                dst = slice(fc * D, (fc + 1) * D)
                i = vector.tensor_tensor(C_w2l[:, dst], B_dq2f[fc % 2][:], A_w2h[:, dst],
                                         ALU.subtract)
                ve.inc(i)
                ve_dq2[fc] = ve.n
        ve_w2 = ve.n

        for grp in range(8):
            tc, sub = grp // 2, grp % 2

            @block.sync
            def _(sync, grp=grp, tc=tc, sub=sub):
                if grp >= 1:
                    ygrp_load(sync, grp, chase=True)

            @block.sync
            def _(sync, grp=grp, tc=tc, sub=sub):
                if grp == 0:
                    sync.wait_ge(ve.h, ve_w2)
                else:
                    sync.wait_ge(ve.h, ve_cmb[grp - 1])
                for jj in range(2):
                    j = tc * 4 + sub * 2 + jj
                    rd(sync, lxa, B_xt2[jj][:], x_h[j * 128:(j + 1) * 128, :])
                    rd(sync, lxa, B_at2[jj][:], hattn[j])
                xa_ld[grp] = lxa.n

            for jj in range(2):
                hg = grp * 2 + jj
                pb = 2 * (hg % 3)

                @block.tensor
                def _(tensor, grp=grp, jj=jj, hg=hg, pb=pb):
                    if hg == 0:
                        tensor.wait_ge(ac.h, ac_y[4 * NF - 1])
                    elif hg >= 3:
                        tensor.wait_ge(ve.h, ve_cmbh[hg - 3])
                    n = 0
                    for fc in range(NF):
                        if jj == 0 and fc % 8 == 0:
                            tensor.wait_ge(lyq[fc // 8].h, yq_ld[(grp, fc // 8)])
                        if hg == 0:
                            tensor.wait_ge(ve.h, ve_dq2[fc])
                        for h2 in range(2):
                            for (a, c) in PAIRS:
                                i = tensor.matmul(
                                    P[pb + h2][:, 0:512],
                                    (B_ygh, B_ygl)[a][:, fc * 256 + jj * 128:fc * 256 + (jj + 1) * 128],
                                    (A_w2h, C_w2l)[c][:, fc * D + h2 * 512:fc * D + (h2 + 1) * 512],
                                    start=(fc == 0 and n % 3 == 0), stop=(fc == NF - 1 and n % 3 == 2))
                                n += 1
                        if jj == 1 and fc % 8 == 7 and fc != NF - 1:
                            mmq.inc(i)
                    mm.inc(i)
                    mm_zh[hg] = mm.n

                @block.vector
                def _(vector, grp=grp, tc=tc, sub=sub, jj=jj, hg=hg, pb=pb):
                    vector.wait_ge(mm.h, mm_zh[hg])
                    vector.wait_ge(lxa.h, xa_ld[grp])
                    if hg == 0:
                        vector.wait_ge(gp.h, gp_g16)
                        vector.wait_ge(ly.h, lybb[0])
                    if hg >= 2:
                        vector.wait_ge(so.h, so_t.get((hg - 2) // 2, 0))
                    j = tc * 4 + sub * 2 + jj
                    for h2 in range(2):
                        hs = slice(h2 * 512, (h2 + 1) * 512)
                        t0 = B_t0[h2]
                        vector.tensor_scalar(t0[:], B_xt2[jj][:, hs], M_mu1[:, j:j + 1],
                                             M_r1[:, j:j + 1], ALU.subtract, ALU.mult)
                        vch(vector, vector.tensor_tensor(B_zb[h2][:], P[pb + h2][:, 0:512],
                                                         M_b2b[:, hs], ALU.add))
                        vch(vector, vector.tensor_tensor(t0[:], t0[:], B_xt2[jj][:, hs], ALU.add))
                        vch(vector, vector.scalar_tensor_tensor(t0[:], B_at2[jj][:, hs],
                                                                M_c16[:, j:j + 1], t0[:],
                                                                ALU.mult, ALU.add))
                        vch(vector, vector.scalar_tensor_tensor(t0[:], M_bvb[:, hs],
                                                                M_g16[:, j:j + 1], t0[:],
                                                                ALU.mult, ALU.add))
                        i = vector.scalar_tensor_tensor(B_ost[jj][:, hs], B_zb[h2][:], 0.5,
                                                        t0[:], ALU.mult, ALU.add)
                    ve.inc(i)
                    ve_cmbh[hg] = ve.n
            ve_cmb[grp] = ve.n
            mm_z[grp] = mm.n

            @block.scalar
            def _(scalar, grp=grp, tc=tc, sub=sub):
                scalar.wait_ge(ve.h, ve_cmb[grp])
                for jj in range(2):
                    j = tc * 4 + sub * 2 + jj
                    wr(scalar, so, out_h[j * 128:(j + 1) * 128, :], B_ost[jj][:])
                so_t[grp] = so.n

        @block.scalar
        def _(scalar):
            scalar.wait_ge(so.h, so.n)

    es.close()
    return nc


_NC_CACHE = [None]


def kernel(L0_attn_x, L0_attn_Wq, L0_attn_bq, L0_attn_Wk, L0_attn_bk, L0_attn_Wv,
           L0_attn_bv, L0_ffn_ln_gamma, L0_ffn_ln_beta, L0_ffn_fc1_q, L0_ffn_fc1_scales,
           L0_ffn_fc1_bias, L0_ffn_fc2_q, L0_ffn_fc2_scales, L0_ffn_fc2_bias):
    x = np.ascontiguousarray(np.asarray(L0_attn_x, dtype=np.float32))

    def emaj(w):
        return np.ascontiguousarray(
            w.reshape(ND, 128, ND, 128).transpose(1, 2, 0, 3).reshape(128, ND * D))

    def h2maj(w):
        return np.ascontiguousarray(
            w.reshape(ND, 128, 2, 512).transpose(1, 2, 0, 3).reshape(128, 2 * 4096))

    wq = emaj(np.asarray(L0_attn_Wq, dtype=np.float32))
    wk = emaj(np.asarray(L0_attn_Wk, dtype=np.float32))
    wv = h2maj(np.asarray(L0_attn_Wv, dtype=np.float32))
    bq = np.asarray(L0_attn_bq, dtype=np.float32)
    bk = np.asarray(L0_attn_bk, dtype=np.float32)
    bv = np.asarray(L0_attn_bv, dtype=np.float32)
    gam = np.asarray(L0_ffn_ln_gamma, dtype=np.float32)
    bet = np.asarray(L0_ffn_ln_beta, dtype=np.float32)
    q1 = np.ascontiguousarray(np.asarray(L0_ffn_fc1_q, dtype=np.int32)
                              .reshape(ND, 128, 8, 512).transpose(0, 2, 1, 3))
    s1 = np.asarray(L0_ffn_fc1_scales, dtype=np.float32)
    b1 = np.asarray(L0_ffn_fc1_bias, dtype=np.float32)
    q2 = np.ascontiguousarray(np.asarray(L0_ffn_fc2_q, dtype=np.int32))
    s2 = np.asarray(L0_ffn_fc2_scales, dtype=np.float32)
    b2 = np.asarray(L0_ffn_fc2_bias, dtype=np.float32)

    bqT = np.ascontiguousarray(bq.reshape(ND, 128).T)
    bkT = np.ascontiguousarray(bk.reshape(ND, 128).T)
    b1T = np.ascontiguousarray(b1.reshape(NF, 128).T)
    gamb = np.ascontiguousarray(np.broadcast_to(gam[None, :], (128, D)))
    betb = np.ascontiguousarray(np.broadcast_to(bet[None, :], (128, D)))
    bvb = np.ascontiguousarray(np.broadcast_to(bv[None, :], (128, D)))
    b2b = np.ascontiguousarray(np.broadcast_to(b2[None, :], (128, D)))
    s1b = np.ascontiguousarray(np.broadcast_to(s1[:, None, :], (ND, 128, FF))
                               .reshape(ND, 128, 8, 512).transpose(0, 2, 1, 3))
    s2b = np.ascontiguousarray(np.broadcast_to(s2[:, None, :], (NF, 128, D)))
    mask = np.where(np.tril(np.ones((128, 128), dtype=bool)), np.float32(0.0),
                    np.float32(NEG_INF)).astype(np.float32)
    eyeb = np.eye(128).astype(ml_dtypes.bfloat16)

    if _NC_CACHE[0] is None:
        _NC_CACHE[0] = build_nc()
    nc = _NC_CACHE[0]

    common = dict(wq=wq, wk=wk, wv=wv, bqT=bqT, bkT=bkT, b1T=b1T, gam=gamb, bet=betb,
                  bvb=bvb, b2b=b2b, q1=q1, s1=s1b, q2=q2, s2=s2b, mask=mask, eyeb=eyeb)
    in_maps = [dict(common, x=np.ascontiguousarray(x[c])) for c in range(8)]
    res = run_bass_kernel_spmd(nc, in_maps, core_ids=list(range(8)))
    out = np.stack([np.asarray(res.results[c]["out"]) for c in range(8)], axis=0)
    return out.astype(np.float32)
